# version 28
# baseline (speedup 1.0000x reference)
import contextlib
import math

import numpy as np
import ml_dtypes

import concourse.bass as bass
from concourse.bass_core_rust import IndirectOffsetOnAxis
import concourse.mybir as mybir
from concourse.bass_utils import run_bass_kernel_spmd

F32 = mybir.dt.float32
BF16 = mybir.dt.bfloat16
I32 = mybir.dt.int32
AF = mybir.ActivationFunctionType
ALU = mybir.AluOpType

D = 2048
NH = 16
DH = 128
T = 2048
HALF = 1024
NFC = 16
NIC = 64
BAND = 8
NB = NIC // BAND
SOFT_SCALE = 1.0 / math.sqrt(128.0)
CSHIFT = -20.0
MASKVAL = -10000.0 * math.sqrt(128.0)

FC_EPS, FC_NHALF, FC_SSCALE, FC_CSHIFT, FC_INV2048, FC_HALF = 0, 1, 2, 3, 4, 5
FC_N = 8

ENGINES = ("pe", "act", "dve", "pool", "sync")
PARANOIA = 0
STAGES_FULL = ("ffn1", "attn", "ffn2")
RG = [[0, 1], [2, 3], [4, 5], [6, 7]]


class Em:
    def __init__(self, sems):
        self.sems = sems
        self.cnt = {k: 0 for k in sems}
        self.ops = {e: [] for e in ENGINES}
        self.floor = {}

    def wait(self, eng, w):
        if w is None:
            return
        sem, val = w
        if val is None or val <= 0:
            return
        assert val <= self.cnt[sem], (
            f"deadlock: {eng} waits {sem}>={val}, emitted {self.cnt[sem]}")
        key = (eng, sem)
        if self.floor.get(key, 0) >= val:
            return
        self.floor[key] = val
        h = self.sems[sem]
        self.ops[eng].append(lambda e, h=h, v=val: e.wait_ge(h, v))

    def waits(self, eng, ws):
        for w in ws:
            self.wait(eng, w)

    SELF_SEM = {"dve": "sV", "act": "sA", "pool": "sP"}

    ENG_INC = {"pe": "sPE", "act": "sA", "dve": "sV", "pool": "sP", "sync": "sP"}

    def op(self, eng, fn, inc=None, inc_n=1, waits=()):
        if PARANOIA == 2:
            for s2 in self.cnt:
                self.wait(eng, (s2, self.cnt[s2]))
            if inc is None:
                inc = self.ENG_INC[eng]
        elif PARANOIA == 3 and eng in ("pe", "act", "dve", "pool"):
            tgt = ("sA", "sV", "sP") if eng == "pe" else ("sPE",)
            for s2 in tgt:
                self.wait(eng, (s2, self.cnt[s2]))
            if inc is None:
                inc = self.ENG_INC[eng]
        elif PARANOIA == 4 and eng in ("pe", "act", "dve", "pool"):
            for s2 in ("sW", "sL", "sS"):
                self.wait(eng, (s2, self.cnt[s2]))
            if inc is None:
                inc = self.ENG_INC[eng]
        elif PARANOIA == 5 and eng in ("pe", "act", "dve", "pool"):
            if inc is None:
                inc = self.ENG_INC[eng]
        elif PARANOIA == 6 and eng in ("pe", "act", "dve", "pool"):
            for s2 in ("sW", "sL", "sS"):
                self.wait(eng, (s2, self.cnt[s2]))
            if inc is None:
                inc = self.ENG_INC[eng]
        ssem = self.SELF_SEM.get(eng)
        if ssem is not None:
            self.wait(eng, (ssem, self.cnt[ssem]))
            if inc is None:
                inc, inc_n = ssem, 1
        self.waits(eng, waits)
        if inc is None:
            self.ops[eng].append(fn)
            return None
        self.cnt[inc] += inc_n
        h = self.sems[inc]
        self.ops[eng].append(lambda e, fn=fn, h=h, n=inc_n: fn(e).then_inc(h, n))
        return self.cnt[inc]

    def dma(self, queue, dst, src, sem, waits=()):
        ws = list(waits)
        if PARANOIA == 2:
            ws += [(s2, self.cnt[s2]) for s2 in self.cnt]
        elif PARANOIA == 4:
            ws += [(s2, self.cnt[s2]) for s2 in ("sPE", "sA", "sV", "sP")]
        elif PARANOIA == 7:
            ws += [(s2, self.cnt[s2]) for s2 in ("sPE", "sA", "sV", "sP")]
        ssem = self.SELF_SEM.get(queue)
        if ssem is not None:
            ws.append((ssem, self.cnt[ssem]))
        self.waits(queue, ws)
        self.cnt[sem] += 16
        h = self.sems[sem]
        self.ops[queue].append(
            lambda e, dst=dst, src=src, h=h: e.dma_start(out=dst, in_=src).then_inc(h, 16))
        return self.cnt[sem]


class Buf:
    def __init__(self):
        self.ready = None
        self.users = []

    def reuse_waits(self):
        ws = list(self.users)
        if self.ready is not None:
            ws.append(self.ready)
        return ws

    def set_ready(self, w):
        self.ready = w
        self.users = []

    def used(self, w):
        if w is not None:
            self.users.append(w)


def build_program(stages=STAGES_FULL):
    nc = bass.Bass("TRN2", target_bir_lowering=False)

    xin = nc.dram_tensor("xin", [D, T], F32, kind="ExternalInput").ap()
    y = nc.dram_tensor("y", [D, T], F32, kind="ExternalOutput").ap()
    wg4 = nc.dram_tensor("wg4", [NIC, 128, NFC, 128], I32, kind="ExternalInput").ap()
    wu4 = nc.dram_tensor("wu4", [NIC, 128, NFC, 128], I32, kind="ExternalInput").ap()
    wd3 = nc.dram_tensor("wd3", [NIC, 128, D], I32, kind="ExternalInput").ap()
    wqkv4 = nc.dram_tensor("wqkv4", [48, 128, NFC, 128], F32, kind="ExternalInput").ap()
    wo4 = nc.dram_tensor("wo4", [NFC, 128, NFC, 128], F32, kind="ExternalInput").ap()
    bqkvT = nc.dram_tensor("bqkvT", [128, 48], F32, kind="ExternalInput").ap()
    boT = nc.dram_tensor("boT", [128, NFC], F32, kind="ExternalInput").ap()
    sgT = nc.dram_tensor("sgT", [128, NIC], F32, kind="ExternalInput").ap()
    suT = nc.dram_tensor("suT", [128, NIC], F32, kind="ExternalInput").ap()
    dsT = nc.dram_tensor("dsT", [128, NFC], F32, kind="ExternalInput").ap()
    idxd = nc.dram_tensor("idxd", [128, 64], I32, kind="ExternalInput").ap()

    kvq_own = [nc.dram_tensor(f"kvq_own{g}", [4096, HALF], BF16) for g in range(2)]
    kvq_g = [nc.dram_tensor(f"kvq_g{g}", [2 * 4096, HALF], BF16) for g in range(2)]
    kvqv_own = [nc.dram_tensor(f"kvqv_own{g}", [2048, HALF], BF16) for g in range(2)]
    kvqv_g = [nc.dram_tensor(f"kvqv_g{g}", [2 * 2048, HALF], BF16) for g in range(2)]
    o_own_a = nc.dram_tensor("o_own_a", [2, 4, 128, T], BF16)
    o_own_b = nc.dram_tensor("o_own_b", [2, 4, 128, T], BF16)
    o_ga = nc.dram_tensor("o_ga", [4, 4, 128, T], BF16)
    o_gb = nc.dram_tensor("o_gb", [4, 4, 128, T], BF16)

    bfdt = ml_dtypes.bfloat16
    ident_c = nc.inline_tensor(np.eye(128, dtype=bfdt), "ident_c")
    selc2_np = np.zeros((128, 4), dtype=bfdt)
    selc2_np[:, 0] = 1.0
    selc2_np[:, 3] = 1.0
    selc2_c = nc.inline_tensor(selc2_np, "selc2_c")
    selc4_np = np.zeros((128, 16), dtype=bfdt)
    for s in range(4):
        selc4_np[:, 4 * s + s] = 1.0
    selc4_c = nc.inline_tensor(selc4_np, "selc4_c")
    selbc_np = np.zeros((4, 512), dtype=bfdt)
    for s in range(4):
        selbc_np[s, 128 * s:128 * (s + 1)] = 1.0
    selbc_c = nc.inline_tensor(selbc_np, "selbc_c")
    selr2_np = np.zeros((2, 256), dtype=bfdt)
    selr2_np[0, 0:128] = 1.0
    selr2_np[1, 128:256] = 1.0
    selr2_c = nc.inline_tensor(selr2_np, "selr2_c")
    negsel_np = np.zeros((2, 256), dtype=bfdt)
    negsel_np[0, 0:128] = -128.0
    negsel_np[1, 128:256] = -128.0
    negsel_c = nc.inline_tensor(negsel_np, "negsel_c")
    fc_np = np.zeros((128, FC_N), dtype=np.float32)
    fc_np[:, FC_EPS] = 1e-5
    fc_np[:, FC_NHALF] = -0.5
    fc_np[:, FC_SSCALE] = SOFT_SCALE
    fc_np[:, FC_CSHIFT] = CSHIFT
    fc_np[:, FC_INV2048] = 1.0 / 2048.0
    fc_np[:, FC_HALF] = 0.5
    fc_c = nc.inline_tensor(fc_np, "fc_c")
    pp = np.arange(128)[:, None]
    qq = np.arange(512)[None, :]
    mk = np.zeros((128, 2048), dtype=np.float32)
    for ks in range(4):
        mk[:, ks * 512:(ks + 1) * 512] = np.where(ks * 128 + pp > qq, MASKVAL, 0.0)
    masks_c = nc.inline_tensor(mk, "masks_c")

    with contextlib.ExitStack() as es:
        sb = lambda n, s, d: es.enter_context(nc.sbuf_tensor(n, s, d))
        ps = lambda n, s, d: es.enter_context(nc.psum_tensor(n, s, d))
        X = sb("X", [128, NFC * HALF], F32)
        xln = sb("xln", [128, NFC * HALF], BF16)
        Hsb = sb("Hsb", [128, 8 * HALF], BF16)
        wgu = sb("wgu", [128, 4 * 2048], BF16)
        wd = sb("wd", [128, 8 * 2048], BF16)
        tmp = sb("tmp", [128, 2 * 512], F32)
        stg = sb("stg", [128, 4 * 512], BF16)
        ptb = sb("ptb", [128, 4 * 512], BF16)
        smf = [sb(f"smf{i}", [4, 512], F32) for i in range(3)]
        smb = [sb(f"smb{i}", [4, 512], BF16) for i in range(5)]
        ident = sb("ident", [128, 128], BF16)
        selc2 = sb("selc2", [128, 4], BF16)
        selc4 = sb("selc4", [128, 16], BF16)
        selbc = sb("selbc", [4, 512], BF16)
        selr2 = sb("selr2", [2, 256], BF16)
        negsel = sb("negsel", [2, 256], BF16)
        fc = sb("fc", [128, FC_N], F32)
        masks = sb("masks", [128, 2048], F32)
        idxt = sb("idxt", [128, 64], I32)
        bqkv = sb("bqkv", [128, 48], F32)
        bo = sb("bo", [128, NFC], F32)
        sg = sb("sg", [128, NIC], F32)
        su = sb("su", [128, NIC], F32)
        ds = sb("ds", [128, NFC], F32)
        dshalf = sb("dshalf", [128, NFC], F32)

        pb = [ps(f"pb{i}", [128, 512], F32) for i in range(7)]
        pvt = ps("pvt", [128, 1024], BF16)

        semnames = ("sW", "sL", "sS", "sPE", "sA", "sV", "sP",
                    "sSL0", "sSL1", "sSL2", "sSL3", "sWd", "sOq",
                    "sKV0", "sKV1", "sKV2", "sKV3", "sKB0", "sKB1",
                    "sSt0", "sSt1", "sSt2", "sSt3",
                    "sCk0", "sCk1", "sCkV0", "sCkV1", "sCo", "sCo2")
        sems = {n: es.enter_context(nc.semaphore(n)) for n in semnames}
        em = Em(sems)

        xw = [None] * NFC
        bankB = [Buf() for _ in range(7)]
        pvtB = [Buf(), Buf()]
        xlnB = Buf()
        hB = [Buf() for _ in range(8)]
        hKB = [Buf(), Buf()]
        wguB = [Buf() for _ in range(4)]
        wdB = [Buf() for _ in range(8)]
        tmpB = [Buf(), Buf()]
        stgB = [Buf() for _ in range(4)]
        ptB = [Buf(), Buf(), Buf(), Buf()]
        smfB = [Buf() for _ in range(3)]
        smbB = [Buf() for _ in range(5)]
        gather_done = {}

        def smf_ap(idx, rows=2):
            return smf[idx][0:rows, :]

        def smb_ap(idx, rows=2):
            return smb[idx][0:rows, :]

        SA, SB_, SC_ = 0, 1, 2
        B_RHI, B_RLO, B_MRHI, B_MRLO, B_XSUM = range(5)

        cv = None
        for dst, src in [
            (ident[:, :], ident_c.ap()), (selc2[:, :], selc2_c.ap()),
            (selc4[:, :], selc4_c.ap()), (selbc[:, :], selbc_c.ap()),
            (selr2[:, :], selr2_c.ap()), (negsel[:, :], negsel_c.ap()),
            (fc[:, :], fc_c.ap()), (masks[:, :], masks_c.ap()),
            (idxt[:, :], idxd),
            (bqkv[:, :], bqkvT), (bo[:, :], boT),
            (sg[:, :], sgT), (su[:, :], suT), (ds[:, :], dsT),
        ]:
            cv = em.dma("sync", dst, src, "sL")
        def x_load(half, first):
            src = xin if first else y
            v = None
            for c in range(NFC):
                ws = [xw[c]] if xw[c] is not None else []
                v = em.dma("sync", X[:, c * HALF:(c + 1) * HALF],
                           src[c * 128:(c + 1) * 128, half * HALF:(half + 1) * HALF],
                           "sL", waits=ws)
            for c in range(NFC):
                xw[c] = ("sL", v)

        def x_store(half):
            v = None
            for c in range(NFC):
                ws = [xw[c]] if xw[c] is not None else []
                v = em.dma("sync", y[c * 128:(c + 1) * 128, half * HALF:(half + 1) * HALF],
                           X[:, c * HALF:(c + 1) * HALF], "sS", waits=ws)
            for c in range(NFC):
                xw[c] = ("sS", v)

        def layernorm():
            stats_v = None
            for c in range(NFC):
                sl_b, sl_q = c % 2, 2 + (c % 2)
                ws = hB[sl_b].reuse_waits() + ([xw[c]] if xw[c] else [])
                vb = em.op("dve", lambda e, sl=sl_b, c=c: e.tensor_copy(
                    out=Hsb[:, sl * HALF:(sl + 1) * HALF],
                    in_=X[:, c * HALF:(c + 1) * HALF]), inc="sV", waits=ws)
                hB[sl_b].set_ready(("sV", vb))
                ws = hB[sl_q].reuse_waits() + ([xw[c]] if xw[c] else [])
                vq = em.op("act", lambda e, sl=sl_q, c=c: e.activation(
                    out=Hsb[:, sl * HALF:(sl + 1) * HALF],
                    in_=X[:, c * HALF:(c + 1) * HALF], func=AF.Square),
                    inc="sA", waits=ws)
                hB[sl_q].set_ready(("sA", vq))
                ws = [("sA", vq), ("sV", vb)]
                if c == 0:
                    ws += bankB[0].reuse_waits() + bankB[1].reuse_waits()
                for t2 in range(2):
                    em.op("pe", lambda e, t2=t2, sl=sl_b, c=c: e.matmul(
                        pb[0][0:2, :], selc2[:, 2 * t2:2 * t2 + 2],
                        Hsb[:, sl * HALF + t2 * 512: sl * HALF + (t2 + 1) * 512],
                        start=(c == 0 and t2 == 0),
                        stop=(c == NFC - 1 and t2 == 1)), waits=ws)
                    ws = []
                    last = (c == NFC - 1 and t2 == 1)
                    v = em.op("pe", lambda e, t2=t2, sl=sl_q, c=c, last=last: e.matmul(
                        pb[1][0:2, :], selc2[:, 2 * t2:2 * t2 + 2],
                        Hsb[:, sl * HALF + t2 * 512: sl * HALF + (t2 + 1) * 512],
                        start=(c == 0 and t2 == 0), stop=last),
                        inc="sPE")
                    stats_v = v
                hB[sl_b].used(("sPE", stats_v))
                hB[sl_q].used(("sPE", stats_v))
            def dve_small(fn, kind, idx, waits):
                tr = smfB if kind == "f" else smbB
                v = em.op("dve", fn, inc="sV", waits=list(waits) + tr[idx].reuse_waits())
                tr[idx].set_ready(("sV", v))
                return v

            v = dve_small(lambda e: e.tensor_copy(out=smf_ap(SA), in_=pb[0][0:2, :]),
                          "f", SA, [("sPE", stats_v)])
            bankB[0].set_ready(None); bankB[0].users = [("sV", v)]
            v = dve_small(lambda e: e.tensor_copy(out=smf_ap(SB_), in_=pb[1][0:2, :]),
                          "f", SB_, [])
            bankB[1].set_ready(None); bankB[1].users = [("sV", v)]
            dve_small(lambda e: e.tensor_scalar_mul(
                out=smf_ap(SC_), in0=smf_ap(SA),
                scalar1=fc[0:2, FC_INV2048:FC_INV2048 + 1]), "f", SC_, [])
            smfB[SA].used(("sV", em.cnt["sV"]))
            dve_small(lambda e: e.tensor_mul(
                out=smf_ap(SA), in0=smf_ap(SC_), in1=smf_ap(SC_)), "f", SA, [])
            var_v = dve_small(lambda e: e.scalar_tensor_tensor(
                out=smf_ap(SB_), in0=smf_ap(SB_),
                scalar=fc[0:2, FC_INV2048:FC_INV2048 + 1],
                in1=smf_ap(SA), op0=ALU.mult, op1=ALU.subtract), "f", SB_, [])
            smfB[SA].used(("sV", var_v))
            v = em.op("act", lambda e: e.activation(
                out=smf_ap(SA), in_=smf_ap(SB_), func=AF.Ln,
                bias=fc[0:2, FC_EPS:FC_EPS + 1]),
                inc="sA", waits=[("sV", var_v)] + smfB[SA].reuse_waits())
            r_v = em.op("act", lambda e: e.activation(
                out=smf_ap(SA), in_=smf_ap(SA), func=AF.Exp,
                scale=fc[0:2, FC_NHALF:FC_NHALF + 1]), inc="sA")
            smfB[SA].set_ready(("sA", r_v))
            smfB[SB_].used(("sA", r_v))
            dve_small(lambda e: e.tensor_mul(
                out=smf_ap(SB_), in0=smf_ap(SC_), in1=smf_ap(SA)), "f", SB_,
                [("sA", r_v)])
            smfB[SC_].used(("sV", em.cnt["sV"]))
            smfB[SA].used(("sV", em.cnt["sV"]))
            split_v = None
            for src_s, hi_b, lo_b in ((SA, B_RHI, B_RLO), (SB_, B_MRHI, B_MRLO)):
                dve_small(lambda e, src_s=src_s, hi_b=hi_b: e.tensor_copy(
                    out=smb_ap(hi_b), in_=smf_ap(src_s)), "b", hi_b, [])
                dve_small(lambda e, hi_b=hi_b: e.tensor_copy(
                    out=smf_ap(SC_), in_=smb_ap(hi_b)), "f", SC_, [])
                split_v = dve_small(lambda e, src_s=src_s, lo_b=lo_b: e.tensor_sub(
                    out=smb_ap(lo_b), in0=smf_ap(src_s), in1=smf_ap(SC_)), "b", lo_b, [])
                smfB[SC_].used(("sV", split_v))
                smfB[src_s].used(("sV", split_v))
            for t2 in range(2):
                ws = ([("sV", split_v)] + bankB[4].reuse_waits()
                      + bankB[5].reuse_waits())
                em.op("pe", lambda e, t2=t2: e.matmul(
                    pb[4][:, :], selr2[:, 128 * t2:128 * (t2 + 1)], smb_ap(B_RHI),
                    start=True, stop=False), waits=ws)
                em.op("pe", lambda e, t2=t2: e.matmul(
                    pb[4][:, :], selr2[:, 128 * t2:128 * (t2 + 1)], smb_ap(B_RLO),
                    start=False, stop=True))
                em.op("pe", lambda e, t2=t2: e.matmul(
                    pb[5][:, :], selr2[:, 128 * t2:128 * (t2 + 1)], smb_ap(B_MRHI),
                    start=True, stop=False))
                bc_v = em.op("pe", lambda e, t2=t2: e.matmul(
                    pb[5][:, :], selr2[:, 128 * t2:128 * (t2 + 1)], smb_ap(B_MRLO),
                    start=False, stop=True), inc="sPE")
                bankB[4].set_ready(("sPE", bc_v))
                bankB[5].set_ready(("sPE", bc_v))
                smbB[B_RHI].used(("sPE", bc_v))
                smbB[B_RLO].used(("sPE", bc_v))
                smbB[B_MRHI].used(("sPE", bc_v))
                smbB[B_MRLO].used(("sPE", bc_v))
                last_tt = None
                for c in range(NFC):
                    ws = [("sPE", bc_v)] + ([xw[c]] if xw[c] else [])
                    if c == 0:
                        ws += xlnB.reuse_waits() + tmpB[0].reuse_waits()
                    em.op("dve", lambda e, c=c, t2=t2: e.tensor_mul(
                        out=tmp[:, 0:512],
                        in0=X[:, c * HALF + t2 * 512: c * HALF + (t2 + 1) * 512],
                        in1=pb[4][:, :]), waits=ws)
                    last_tt = em.op("dve", lambda e, c=c, t2=t2: e.tensor_sub(
                        out=xln[:, c * HALF + t2 * 512: c * HALF + (t2 + 1) * 512],
                        in0=tmp[:, 0:512], in1=pb[5][:, :]), inc="sV")
                bankB[4].used(("sV", last_tt))
                bankB[5].used(("sV", last_tt))
                if t2 == 0:
                    xlnB.set_ready(("sV", last_tt))
                    tmpB[0].set_ready(("sV", last_tt))
                else:
                    xlnB.ready = ("sV", last_tt)
                    tmpB[0].ready = ("sV", last_tt)
            xs_v = None
            ws = bankB[2].reuse_waits() + [xlnB.ready]
            for c in range(NFC):
                for t2 in range(2):
                    last = (c == NFC - 1 and t2 == 1)
                    xs_v = em.op("pe", lambda e, c=c, t2=t2, last=last: e.matmul(
                        pb[2][0:2, :], selc2[:, 2 * t2:2 * t2 + 2],
                        xln[:, c * HALF + t2 * 512: c * HALF + (t2 + 1) * 512],
                        start=(c == 0 and t2 == 0), stop=last),
                        inc=("sPE" if last else None), waits=ws) or xs_v
                    ws = []
            v = em.op("dve", lambda e: e.tensor_copy(out=smb_ap(B_XSUM), in_=pb[2][0:2, :]),
                      inc="sV", waits=[("sPE", xs_v)] + smbB[B_XSUM].reuse_waits())
            smbB[B_XSUM].set_ready(("sV", v))
            bankB[2].set_ready(None); bankB[2].users = [("sV", v)]

        def load_gu(i):
            gslot, uslot = i % 2, 2 + (i % 2)
            for slot, src in ((gslot, wg4), (uslot, wu4)):
                ws = wguB[slot].reuse_waits()
                v = em.dma("pool", wgu[:, slot * 2048:(slot + 1) * 2048],
                           src[i].rearrange("p a b -> p (a b)"), f"sSL{slot}", waits=ws)
                wguB[slot].set_ready((f"sSL{slot}", v))

        def ffn(after_band_hook=None, early_hook=None):
            xln_r = xlnB.ready
            xsum_r = smbB[B_XSUM].ready
            load_gu(0)
            load_gu(1)
            uidx = 0
            for b in range(NB):
                dn_state = {"i": 0, "v": None}

                def emit_down(k):
                    for _ in range(k):
                        ii = dn_state["i"]
                        if ii >= BAND:
                            return
                        dn_state["i"] += 1
                        i = b * BAND + ii
                        ws = wdB[ii].reuse_waits()
                        dn_state["v"] = em.dma(
                            "pool", wd[:, ii * 2048:(ii + 1) * 2048], wd3[i],
                            "sWd", waits=ws)
                if b == 0 and early_hook is not None:
                    emit_down(BAND)
                    early_hook()
                hs_mm_v = None
                hs_pend = []

                def flush_hsum():
                    ws, ii, t2, last = hs_pend.pop(0)
                    return em.op("pe", lambda e, ii=ii, t2=t2, last=last: e.matmul(
                        pb[6][0:2, :], selc2[:, 2 * t2:2 * t2 + 2],
                        Hsb[:, ii * HALF + t2 * 512: ii * HALF + (t2 + 1) * 512],
                        start=(ii == 0 and t2 == 0), stop=last),
                        inc=("sPE" if last else None), waits=ws)

                for ii in range(BAND):
                    i = b * BAND + ii
                    gslot, uslot = i % 2, 2 + (i % 2)
                    for t2 in range(2):
                        par = uidx % 2
                        gb, ub = par, 2 + par
                        ws = [wguB[gslot].ready, xln_r, xsum_r] + bankB[gb].reuse_waits()
                        for hc in range(NFC):
                            em.op("pe", lambda e, gslot=gslot, hc=hc, t2=t2, gb=gb: e.matmul(
                                pb[gb][:, :],
                                wgu[:, gslot * 2048 + hc * 128: gslot * 2048 + (hc + 1) * 128],
                                xln[:, hc * HALF + t2 * 512: hc * HALF + (t2 + 1) * 512],
                                start=(hc == 0), stop=False), waits=ws)
                            ws = []
                        g_v = em.op("pe", lambda e, t2=t2, gb=gb: e.matmul(
                            pb[gb][:, :], negsel[:, 128 * t2:128 * (t2 + 1)],
                            smb_ap(B_XSUM), start=False, stop=True), inc="sPE")
                        bankB[gb].set_ready(("sPE", g_v))
                        wguB[gslot].used(("sPE", g_v))
                        ws = [wguB[uslot].ready] + bankB[ub].reuse_waits()
                        for hc in range(NFC):
                            em.op("pe", lambda e, uslot=uslot, hc=hc, t2=t2, ub=ub: e.matmul(
                                pb[ub][:, :],
                                wgu[:, uslot * 2048 + hc * 128: uslot * 2048 + (hc + 1) * 128],
                                xln[:, hc * HALF + t2 * 512: hc * HALF + (t2 + 1) * 512],
                                start=(hc == 0), stop=False), waits=ws)
                            ws = []
                        u_v = em.op("pe", lambda e, t2=t2, ub=ub: e.matmul(
                            pb[ub][:, :], negsel[:, 128 * t2:128 * (t2 + 1)],
                            smb_ap(B_XSUM), start=False, stop=True), inc="sPE")
                        bankB[ub].set_ready(("sPE", u_v))
                        wguB[uslot].used(("sPE", u_v))
                        tslot = par
                        ws = [("sPE", g_v)] + tmpB[tslot].reuse_waits()
                        silu_v = em.op("act", lambda e, gb=gb, i=i, tslot=tslot: e.activation(
                            out=tmp[:, tslot * 512:(tslot + 1) * 512], in_=pb[gb][:, :],
                            func=AF.Silu, scale=sg[:, i:i + 1]), inc="sA", waits=ws)
                        tmpB[tslot].set_ready(("sA", silu_v))
                        bankB[gb].used(("sA", silu_v))
                        ws = [("sA", silu_v), ("sPE", u_v)] + hB[ii].reuse_waits()
                        h_v = em.op("dve", lambda e, ub=ub, i=i, ii=ii, t2=t2, tslot=tslot:
                                    e.scalar_tensor_tensor(
                                        out=Hsb[:, ii * HALF + t2 * 512: ii * HALF + (t2 + 1) * 512],
                                        in0=pb[ub][:, :], scalar=su[:, i:i + 1],
                                        in1=tmp[:, tslot * 512:(tslot + 1) * 512],
                                        op0=ALU.mult, op1=ALU.mult), inc="sV", waits=ws)
                        if t2 == 0:
                            hB[ii].set_ready(("sV", h_v))
                        else:
                            hB[ii].ready = ("sV", h_v)
                        bankB[ub].used(("sV", h_v))
                        tmpB[tslot].used(("sV", h_v))
                        ws = [("sV", h_v)]
                        if ii == 0 and t2 == 0:
                            ws += bankB[6].reuse_waits()
                        last = (ii == BAND - 1 and t2 == 1)
                        hs_pend.append((ws, ii, t2, last))
                        if len(hs_pend) > 2:
                            hs_mm_v = flush_hsum() or hs_mm_v
                        uidx += 1
                        if t2 == 1 and i + 2 < NIC and (i + 2) // BAND == b:
                            load_gu(i + 2)
                    emit_down(2)
                    if after_band_hook is not None:
                        after_band_hook(b)
                    if ii == BAND - 1 and b + 1 < NB:
                        load_gu(b * BAND + BAND)
                        load_gu(b * BAND + BAND + 1)
                while hs_pend:
                    hs_mm_v = flush_hsum() or hs_mm_v
                emit_down(BAND)
                bankB[6].set_ready(("sPE", hs_mm_v))
                v = em.op("dve", lambda e: e.tensor_copy(out=smf_ap(SA), in_=pb[6][0:2, :]),
                          inc="sV", waits=[("sPE", hs_mm_v)] + smfB[SA].reuse_waits())
                smfB[SA].set_ready(("sV", v))
                bankB[6].used(("sV", v))
                v = em.op("dve", lambda e: e.tensor_copy(out=smb_ap(B_RHI), in_=smf_ap(SA)),
                          inc="sV", waits=smbB[B_RHI].reuse_waits())
                smbB[B_RHI].set_ready(("sV", v))
                v = em.op("dve", lambda e: e.tensor_copy(out=smf_ap(SB_), in_=smb_ap(B_RHI)),
                          inc="sV", waits=smfB[SB_].reuse_waits())
                smfB[SB_].set_ready(("sV", v))
                hs_v = em.op("dve", lambda e: e.tensor_sub(
                    out=smb_ap(B_RLO), in0=smf_ap(SA), in1=smf_ap(SB_)),
                    inc="sV", waits=smbB[B_RLO].reuse_waits())
                smbB[B_RLO].set_ready(("sV", hs_v))
                smfB[SA].used(("sV", hs_v))
                smfB[SB_].used(("sV", hs_v))
                for ii in range(BAND):
                    wdB[ii].set_ready(("sWd", dn_state["v"]))
                vidx = 0
                for hf in range(NFC):
                    for t2 in range(2):
                        yb = 4 + (vidx % 2)
                        ws = bankB[yb].reuse_waits()
                        for ii in range(BAND):
                            ws2 = ws + [wdB[ii].ready, hB[ii].ready]
                            em.op("pe", lambda e, ii=ii, hf=hf, t2=t2, yb=yb: e.matmul(
                                pb[yb][:, :],
                                wd[:, ii * 2048 + hf * 128: ii * 2048 + (hf + 1) * 128],
                                Hsb[:, ii * HALF + t2 * 512: ii * HALF + (t2 + 1) * 512],
                                start=(ii == 0), stop=False), waits=ws2)
                            ws = []
                        em.op("pe", lambda e, t2=t2, yb=yb: e.matmul(
                            pb[yb][:, :], negsel[:, 128 * t2:128 * (t2 + 1)],
                            smb_ap(B_RHI), start=False, stop=False),
                            waits=[("sV", hs_v)])
                        y_v = em.op("pe", lambda e, t2=t2, yb=yb: e.matmul(
                            pb[yb][:, :], negsel[:, 128 * t2:128 * (t2 + 1)],
                            smb_ap(B_RLO), start=False, stop=True), inc="sPE")
                        bankB[yb].set_ready(("sPE", y_v))
                        for ii in range(BAND):
                            wdB[ii].used(("sPE", y_v))
                            hB[ii].used(("sPE", y_v))
                        smbB[B_RHI].used(("sPE", y_v))
                        smbB[B_RLO].used(("sPE", y_v))
                        ws = [("sPE", y_v)] + ([xw[hf]] if xw[hf] else [])
                        xv = em.op("dve", lambda e, hf=hf, t2=t2, yb=yb: e.scalar_tensor_tensor(
                            out=X[:, hf * HALF + t2 * 512: hf * HALF + (t2 + 1) * 512],
                            in0=pb[yb][:, :], scalar=dshalf[:, hf:hf + 1],
                            in1=X[:, hf * HALF + t2 * 512: hf * HALF + (t2 + 1) * 512],
                            op0=ALU.mult, op1=ALU.add), inc="sV", waits=ws)
                        xw[hf] = ("sV", xv)
                        bankB[yb].used(("sV", xv))
                        vidx += 1
            blanket = ("sPE", em.cnt["sPE"])
            xlnB.used(blanket)
            smbB[B_XSUM].used(blanket)

        def qkv(g, defer_gather=False):
            xln_r = xlnB.ready
            order = list(range(16, 32)) + list(range(0, 16)) + list(range(32, 48))
            uidx = 0
            vt_pend = []

            def flush_vt():
                vu, ev_v, st_slot, head, t2 = vt_pend.pop(0)
                ph = vu % 2
                ws = [("sA", ev_v)] + pvtB[ph].reuse_waits()
                tp_v = None
                for j in range(4):
                    tp_v = em.op("pe", lambda e, st_slot=st_slot, j=j, ph=ph: e.transpose(
                        pvt[:, ph * 512 + j * 128: ph * 512 + (j + 1) * 128],
                        stg[:, st_slot * 512 + j * 128: st_slot * 512 + (j + 1) * 128],
                        ident[:, :]),
                        inc=("sPE" if j == 3 else None), waits=ws) or tp_v
                    ws = []
                pvtB[ph].set_ready(("sPE", tp_v))
                stgB[st_slot].used(("sPE", tp_v))
                st2 = (vu + 2) % 4
                ws = [("sPE", tp_v)] + stgB[st2].reuse_waits()
                e2_v = em.op("dve", lambda e, ph=ph, st2=st2: e.tensor_copy(
                    out=stg[:, st2 * 512:(st2 + 1) * 512],
                    in_=pvt[:, ph * 512:(ph + 1) * 512]), inc="sV", waits=ws)
                stgB[st2].set_ready(("sV", e2_v))
                pvtB[ph].used(("sV", e2_v))
                fshard, f0 = head // 8, (head % 8) * 128
                dst = (kvqv_own[g].ap()
                       [fshard * HALF + t2 * 512:
                        fshard * HALF + (t2 + 1) * 512, f0:f0 + 128]
                       .rearrange("(j p) d -> p j d", p=128))
                dv = em.dma("sync", dst, stg[:, st2 * 512:(st2 + 1) * 512],
                            f"sSt{st2}", waits=[("sV", e2_v)])
                stgB[st2].used((f"sSt{st2}", dv))

            for n, oc in enumerate(order):
                slot = n % 4
                ws = wguB[slot].reuse_waits()
                v = em.dma("pool", wgu[:, slot * 2048:(slot + 1) * 2048],
                           wqkv4[oc].rearrange("p a b -> p (a b)"), f"sSL{slot}", waits=ws)
                wguB[slot].set_ready((f"sSL{slot}", v))
                for t2 in range(2):
                    par = uidx % 2
                    ws = [wguB[slot].ready, xln_r] + bankB[par].reuse_waits()
                    mm_v = None
                    for hc in range(NFC):
                        mm_v = em.op("pe", lambda e, slot=slot, hc=hc, t2=t2, par=par: e.matmul(
                            pb[par][:, :],
                            wgu[:, slot * 2048 + hc * 128: slot * 2048 + (hc + 1) * 128],
                            xln[:, hc * HALF + t2 * 512: hc * HALF + (t2 + 1) * 512],
                            start=(hc == 0), stop=(hc == NFC - 1)),
                            inc=("sPE" if hc == NFC - 1 else None), waits=ws) or mm_v
                        ws = []
                    bankB[par].set_ready(("sPE", mm_v))
                    wguB[slot].used(("sPE", mm_v))
                    st_slot = uidx % 4
                    ws = [("sPE", mm_v)] + stgB[st_slot].reuse_waits()
                    ev_v = em.op("act", lambda e, par=par, oc=oc, st_slot=st_slot: e.activation(
                        out=stg[:, st_slot * 512:(st_slot + 1) * 512], in_=pb[par][:, :],
                        func=AF.Identity, bias=bqkv[:, oc:oc + 1]), inc="sA", waits=ws)
                    stgB[st_slot].set_ready(("sA", ev_v))
                    bankB[par].used(("sA", ev_v))
                    which, head = divmod(oc, 16)
                    if which in (0, 1):
                        base = (0 if which == 1 else 2048) + head * 128
                        dv = em.dma("sync",
                                    kvq_own[g].ap()[base:base + 128,
                                                    t2 * 512:(t2 + 1) * 512],
                                    stg[:, st_slot * 512:(st_slot + 1) * 512],
                                    f"sSt{st_slot}", waits=[("sA", ev_v)])
                        stgB[st_slot].used((f"sSt{st_slot}", dv))
                    else:
                        vt_pend.append((uidx, ev_v, st_slot, head, t2))
                        if len(vt_pend) > 1:
                            flush_vt()
                    uidx += 1
                if n == 31:
                    stw_kq = [(f"sSt{j}", em.cnt[f"sSt{j}"]) for j in range(4)]
                if n == 47:
                    while vt_pend:
                        flush_vt()
            stw = [(f"sSt{j}", em.cnt[f"sSt{j}"]) for j in range(4)]

            def emit_gather(queue="pool"):
                cv = em.op(queue, lambda e, g=g: e.collective_compute(
                    "AllGather", ALU.bypass, replica_groups=RG,
                    ins=[kvq_own[g].ap()], outs=[kvq_g[g].ap()]),
                    inc=f"sCk{g}", waits=stw_kq)
                gather_done[("kvq", g)] = (f"sCk{g}", cv)
                cv = em.op(queue, lambda e, g=g: e.collective_compute(
                    "AllGather", ALU.bypass, replica_groups=RG,
                    ins=[kvqv_own[g].ap()], outs=[kvqv_g[g].ap()]),
                    inc=f"sCkV{g}", waits=stw)
                gather_done[("kvqV", g)] = (f"sCkV{g}", cv)
            if not defer_gather:
                emit_gather()
            xlnB.used(("sPE", em.cnt["sPE"]))
            return emit_gather

        def kc_map(kc):
            return ((kc % 4) // 2, kc // 4, (kc % 2) * 512)

        def attention(pre_hook=None, mid_hook=None):
            pvtf = pvt.ap().bitcast(F32)
            lB = pvtB[0]
            pair_order = [(0, 0), (0, 1), (1, 0), (1, 1)]
            pair_order += [(0, qs) for qs in range(2, 8)]
            pair_order += [(1, qs) for qs in range(2, 8)]

            def igather(queue, out_ap, g, col, eoff, sem, waits, vten=False):
                em.waits(queue, waits)
                em.cnt[sem] += 16
                h = em.sems[sem]
                flat = (kvqv_g if vten else kvq_g)[g].ap()
                em.ops[queue].append(
                    lambda e, out_ap=out_ap, flat=flat, col=col, eoff=eoff, h=h:
                    e.indirect_dma_start(
                        out=out_ap, out_offset=None, in_=flat,
                        in_offset=IndirectOffsetOnAxis(ap=idxt[:, col:col + 1], axis=0),
                        element_offset=eoff).then_inc(h, 16))
                return em.cnt[sem]

            def emit_q(gi):
                hg, qs = pair_order[2 * gi]
                gq, jq, _ = kc_map(qs)
                P = gi % 2
                ws = (wguB[2 * P].reuse_waits() + wguB[2 * P + 1].reuse_waits()
                      + [gather_done[("kvq", gq)]])
                qv = None
                for s in range(4):
                    qv = igather("pool",
                                 wgu[:, P * 4096 + s * 1024: P * 4096 + (s + 1) * 1024],
                                 gq, 16 + jq * 8 + hg * 4 + s, 0, f"sSL{2 * P}", ws)
                    ws = []
                wguB[2 * P].set_ready((f"sSL{2 * P}", qv))
                wguB[2 * P + 1].set_ready((f"sSL{2 * P}", qv))

            q_emitted = set()

            def maybe_emit_q(pi):
                gi = pi // 2
                if 2 * gi < len(pair_order) and gi not in q_emitted:
                    g_need = kc_map(pair_order[2 * gi][1])[0]
                    if ("kvq", g_need) in gather_done:
                        q_emitted.add(gi)
                        emit_q(gi)

            for blk in range(2):
                us = []
                for sl in range(4 * blk, 4 * blk + 4):
                    us += hB[sl].reuse_waits()
                hKB[blk].users = us
                hKB[blk].ready = None

            fin_pend = []

            def flush_fin():
                while fin_pend:
                    fin_pend.pop(0)()

            def make_finalize(av_v, hg, qs, pi):
                def fin():
                    ws = [("sPE", av_v)] + smfB[SA].reuse_waits()
                    v = em.op("dve", lambda e: e.tensor_copy(
                        out=smf_ap(SA, rows=4), in_=pvtf[0:4, 0:512]),
                        inc="sV", waits=ws)
                    smfB[SA].set_ready(("sV", v))
                    lB.used(("sV", v))
                    v = em.op("dve", lambda e: e.reciprocal(
                        out=smf_ap(SB_, rows=4), in_=smf_ap(SA, rows=4)),
                        inc="sV", waits=smfB[SB_].reuse_waits())
                    smfB[SB_].set_ready(("sV", v))
                    smfB[SA].used(("sV", v))
                    lb = B_RHI
                    lb_v = em.op("dve", lambda e, lb=lb: e.tensor_copy(
                        out=smb_ap(lb, rows=4), in_=smf_ap(SB_, rows=4)),
                        inc="sV", waits=smbB[lb].reuse_waits())
                    smbB[lb].set_ready(("sV", lb_v))
                    smfB[SB_].used(("sV", lb_v))
                    for s in range(4):
                        bcb = s % 2
                        ws = [("sV", lb_v)] + bankB[bcb].reuse_waits()
                        bc_v = em.op("pe", lambda e, s=s, bcb=bcb, lb=lb: e.matmul(
                            pb[bcb][:, :], selbc[:, 128 * s:128 * (s + 1)],
                            smb_ap(lb, rows=4), start=True, stop=True),
                            inc="sPE", waits=ws)
                        bankB[bcb].set_ready(("sPE", bc_v))
                        smbB[lb].used(("sPE", bc_v))
                        tslot = s % 2
                        ws = [("sPE", bc_v)] + tmpB[tslot].reuse_waits()
                        cp_v = em.op("dve", lambda e, bcb=bcb, tslot=tslot: e.tensor_copy(
                            out=tmp[:, tslot * 512:(tslot + 1) * 512], in_=pb[bcb][:, :]),
                            inc="sV", waits=ws)
                        tmpB[tslot].set_ready(("sV", cp_v))
                        bankB[bcb].used(("sV", cp_v))
                        st_slot = s
                        ob = 3 + s
                        ws = stgB[st_slot].reuse_waits()
                        nm_v = em.op("dve", lambda e, ob=ob, tslot=tslot, st_slot=st_slot: e.tensor_mul(
                            out=stg[:, st_slot * 512:(st_slot + 1) * 512],
                            in0=pb[ob][:, :], in1=tmp[:, tslot * 512:(tslot + 1) * 512]),
                            inc="sV", waits=ws)
                        stgB[st_slot].set_ready(("sV", nm_v))
                        tmpB[tslot].used(("sV", nm_v))
                        bankB[ob].used(("sV", nm_v))
                        oo = o_own_a if hg == 0 else o_own_b
                        dv = em.dma("sync",
                                    oo.ap()[qs // 4, s][:, (qs % 4) * 512:(qs % 4) * 512 + 512],
                                    stg[:, st_slot * 512:(st_slot + 1) * 512],
                                    f"sSt{st_slot}", waits=[("sV", nm_v)])
                        stgB[st_slot].used((f"sSt{st_slot}", dv))
                    if pi == 9:
                        stw = [(f"sSt{j}", em.cnt[f"sSt{j}"]) for j in range(4)]
                        cv = em.op("pool", lambda e: e.collective_compute(
                            "AllGather", ALU.bypass, replica_groups=RG,
                            ins=[o_own_a.ap()], outs=[o_ga.ap()]),
                            inc="sCo", waits=stw)
                        gather_done[("o", "a")] = ("sCo", cv)
                return fin

            cur_kb = {}
            cur_v = {}

            def load_kblock(kb, hgl, gk, jk):
                if cur_kb.get(kb) == (hgl, gk, jk):
                    return
                ws = hKB[kb].reuse_waits() + [gather_done[("kvq", gk)]]
                kkv = None
                for s in range(4):
                    kkv = igather(
                        "pool",
                        Hsb[:, kb * 4096 + s * 1024: kb * 4096 + (s + 1) * 1024],
                        gk, jk * 8 + hgl * 4 + s, 0, f"sKB{kb}", ws)
                    ws = []
                hKB[kb].set_ready((f"sKB{kb}", kkv))
                cur_kb[kb] = (hgl, gk, jk)

            def load_v(r, hgl, gk, jk, ckh):
                if cur_v.get(r) == (hgl, gk, jk, ckh):
                    return
                vsl = (2 * r, 2 * r + 1)
                ws = (wdB[vsl[0]].reuse_waits() + wdB[vsl[1]].reuse_waits()
                      + [gather_done[("kvqV", gk)]])
                kv = None
                for ks in range(4):
                    kv = igather("pool",
                                 wd[:, r * 2048 + ks * 512: r * 2048 + (ks + 1) * 512],
                                 gk, 32 + jk * 8 + ckh * 4 + ks,
                                 hgl * 512, f"sKV{r}", ws, vten=True)
                    ws = []
                for sl in vsl:
                    wdB[sl].set_ready((f"sKV{r}", kv))
                cur_v[r] = (hgl, gk, jk, ckh)

            for hgE in (0, 1):
                load_kblock(hgE, hgE, 0, 0)
                maybe_emit_q(2 * hgE)
                for ckh in (0, 1):
                    load_v(2 * hgE + ckh, hgE, 0, 0, ckh)
            if pre_hook is not None:
                pre_hook()
            for pi, (hg, qs) in enumerate(pair_order):
                if mid_hook is not None and pi in mid_hook:
                    mid_hook[pi]()
                maybe_emit_q(pi)
                P2 = (pi // 2) % 2
                cq2 = (qs % 2) * 512
                tile = 0
                nkc = qs + 1
                avl_pend = []
                lv_last = None

                def flush_avl():
                    nonlocal lv_last
                    (ex_v, r, ks, s, ptp, first, last) = avl_pend.pop(0)
                    ob = 3 + s
                    ws = [("sA", ex_v)]
                    if first:
                        ws = ws + bankB[ob].reuse_waits()
                        if s == 0:
                            ws = ws + lB.reuse_waits()
                    em.op("pe", lambda e, r=r, ks=ks, s=s, ptp=ptp, first=first, last=last: e.matmul(
                        pb[3 + s][:, :],
                        wd[:, r * 2048 + ks * 512 + s * 128: r * 2048 + ks * 512 + (s + 1) * 128],
                        ptb[:, ptp * 512:(ptp + 1) * 512],
                        start=first, stop=last), waits=ws)
                    l_v = em.op("pe", lambda e, s=s, ptp=ptp, first=first, last=last: e.matmul(
                        pvtf[0:4, 0:512], selc4[:, 4 * s:4 * s + 4],
                        ptb[:, ptp * 512:(ptp + 1) * 512],
                        start=(first and s == 0), stop=(last and s == 3)),
                        inc="sPE")
                    ptB[ptp].used(("sPE", l_v))
                    wdB[2 * r].used(("sPE", l_v))
                    wdB[2 * r + 1].used(("sPE", l_v))
                    if first:
                        bankB[ob].set_ready(("sPE", l_v))
                        if s == 0:
                            lB.set_ready(("sPE", l_v))
                    lv_last = l_v

                for kc in range(nkc):
                    gk, jk, ck = kc_map(kc)
                    if pi < 4:
                        kb = hg
                        r = 2 * hg + (kc % 2)
                    else:
                        kb = (kc // 2) % 2
                        r = kc % 4
                        load_kblock(kb, hg, gk, jk)
                        load_v(r, hg, gk, jk, ck // 512)
                    vslots = (2 * r, 2 * r + 1)
                    if kc == min(1, nkc - 1):
                        maybe_emit_q(pi + 1)
                    diag = (kc == qs)
                    for ks in range(4):
                        for s in range(4):
                            if tile == 3:
                                flush_fin()
                            par = tile % 3
                            ws = ([hKB[kb].ready, wguB[2 * P2].ready, wdB[vslots[0]].ready]
                                  + bankB[par].reuse_waits())
                            smm_v = em.op("pe", lambda e, kb=kb, ck=ck, s=s, ks=ks,
                                          P2=P2, cq2=cq2, par=par: e.matmul(
                                pb[par][:, :],
                                Hsb[:, kb * 4096 + s * 1024 + ck + ks * 128:
                                    kb * 4096 + s * 1024 + ck + (ks + 1) * 128],
                                wgu[:, P2 * 4096 + s * 1024 + cq2:
                                    P2 * 4096 + s * 1024 + cq2 + 512],
                                start=True, stop=True), inc="sPE", waits=ws)
                            bankB[par].set_ready(("sPE", smm_v))
                            exp_w = [("sPE", smm_v)]
                            if diag:
                                mk_v = em.op("dve", lambda e, par=par, ks=ks: e.tensor_add(
                                    out=pb[par][:, :], in0=pb[par][:, :],
                                    in1=masks[:, ks * 512:(ks + 1) * 512]),
                                    inc="sV", waits=[("sPE", smm_v)])
                                exp_w = [("sV", mk_v)]
                            ptp = tile % 4
                            ws = exp_w + ptB[ptp].reuse_waits()
                            ex_v = em.op("act", lambda e, par=par, ptp=ptp: e.activation(
                                out=ptb[:, ptp * 512:(ptp + 1) * 512], in_=pb[par][:, :],
                                func=AF.Exp, scale=fc[:, FC_SSCALE:FC_SSCALE + 1],
                                bias=fc[:, FC_CSHIFT:FC_CSHIFT + 1]),
                                inc="sA", waits=ws)
                            ptB[ptp].set_ready(("sA", ex_v))
                            bankB[par].used(("sA", ex_v))
                            first = (kc == 0 and ks == 0)
                            last = (kc == nkc - 1 and ks == 3)
                            avl_pend.append((ex_v, r, ks, s, ptp, first, last))
                            if len(avl_pend) > 3:
                                flush_avl()
                            tile += 1
                    hold = ("sPE", em.cnt["sPE"])
                    hKB[kb].used(hold)
                    for sl in vslots:
                        wdB[sl].used(hold)
                while avl_pend:
                    flush_avl()
                hold = ("sPE", em.cnt["sPE"])
                wguB[2 * P2].used(hold)
                wguB[2 * P2 + 1].used(hold)
                fin_pend.append(make_finalize(lv_last, hg, qs, pi))
            flush_fin()
            blanket = ("sPE", em.cnt["sPE"])
            for sl in range(8):
                hB[sl].used(blanket)
                wdB[sl].used(blanket)
            wo_oq_loads(0, True)
            stw = [(f"sSt{j}", em.cnt[f"sSt{j}"]) for j in range(4)]
            cv = em.op("pool", lambda e: e.collective_compute(
                "AllGather", ALU.bypass, replica_groups=RG,
                ins=[o_own_b.ap()], outs=[o_gb.ap()]),
                inc="sCo2", waits=stw)
            gather_done[("o", "b")] = ("sCo2", cv)

        oq_B = [Buf() for _ in range(32)]

        oq_loaded = {}

        def wo_oq_loads(half, part_a):
            if (half, part_a) in oq_loaded:
                return
            oq_loaded[(half, part_a)] = True
            if half not in oq_loaded:
                oq_loaded[half] = True
                for slot in range(32):
                    b2 = Buf()
                    b2.users = wdB[slot // 4].reuse_waits() + oq_B[slot].reuse_waits()
                    oq_B[slot] = b2
            v = None
            for qli in range(2):
                ql = half * 2 + qli
                for fcx in range(NFC):
                    if ((fcx % 8) < 4) != part_a:
                        continue
                    slot = qli * 16 + fcx
                    og = o_ga if part_a else o_gb
                    gw = gather_done[("o", "a" if part_a else "b")]
                    ws = oq_B[slot].reuse_waits() + [gw]
                    em.waits("pool", ws)
                    em.cnt["sOq"] += 16
                    h = em.sems["sOq"]
                    flat = og.ap().rearrange("a b p t -> (a b p) t")
                    em.ops["pool"].append(
                        lambda e, slot=slot, flat=flat, fcx=fcx, ql=ql, h=h:
                        e.indirect_dma_start(
                            out=wd[:, slot * 512:(slot + 1) * 512], out_offset=None,
                            in_=flat,
                            in_offset=IndirectOffsetOnAxis(ap=idxt[:, 48 + fcx:49 + fcx], axis=0),
                            element_offset=ql * 512).then_inc(h, 16))
                    v = em.cnt["sOq"]
            for qli in range(2):
                for fcx in range(NFC):
                    if ((fcx % 8) < 4) == part_a:
                        oq_B[qli * 16 + fcx].set_ready(("sOq", v))

        def wo_proj(half):
            wo_oq_loads(half, True)
            wo_oq_loads(half, False)
            passes = ([[f for f in range(NFC) if (f % 8) < 4],
                       [f for f in range(NFC) if (f % 8) >= 4]]
                      if half == 0 else [list(range(NFC))])
            for pn, fcx_list in enumerate(passes):
                for ofc in range(NFC):
                    wslot = 2 + (ofc % 2)
                    ws = wguB[wslot].reuse_waits()
                    v = em.dma("pool", wgu[:, wslot * 2048:(wslot + 1) * 2048],
                               wo4[ofc].rearrange("p a b -> p (a b)"), f"sSL{wslot}", waits=ws)
                    wguB[wslot].set_ready((f"sSL{wslot}", v))
                    for qli in range(2):
                        t2 = qli
                        par = (ofc * 2 + qli) % 2
                        ws = [wguB[wslot].ready] + bankB[par].reuse_waits()
                        mm_v = None
                        nlast = len(fcx_list) - 1
                        for nf, fcx in enumerate(fcx_list):
                            slot = qli * 16 + fcx
                            ws2 = ws + [oq_B[slot].ready]
                            mm_v = em.op("pe", lambda e, wslot=wslot, fcx=fcx, slot=slot, par=par,
                                         nf=nf, nlast=nlast: e.matmul(
                                pb[par][:, :],
                                wgu[:, wslot * 2048 + fcx * 128: wslot * 2048 + (fcx + 1) * 128],
                                wd[:, slot * 512:(slot + 1) * 512],
                                start=(nf == 0), stop=(nf == nlast)),
                                inc=("sPE" if nf == nlast else None), waits=ws2) or mm_v
                            ws = []
                        bankB[par].set_ready(("sPE", mm_v))
                        wguB[wslot].used(("sPE", mm_v))
                        for fcx in fcx_list:
                            oq_B[qli * 16 + fcx].used(("sPE", mm_v))
                        ws = [("sPE", mm_v)] + ([xw[ofc]] if xw[ofc] else [])
                        if pn == 0:
                            xv = em.op("dve", lambda e, ofc=ofc, t2=t2, par=par: e.scalar_tensor_tensor(
                                out=X[:, ofc * HALF + t2 * 512: ofc * HALF + (t2 + 1) * 512],
                                in0=pb[par][:, :], scalar=bo[:, ofc:ofc + 1],
                                in1=X[:, ofc * HALF + t2 * 512: ofc * HALF + (t2 + 1) * 512],
                                op0=ALU.add, op1=ALU.add), inc="sV", waits=ws)
                        else:
                            xv = em.op("dve", lambda e, ofc=ofc, t2=t2, par=par: e.tensor_add(
                                out=X[:, ofc * HALF + t2 * 512: ofc * HALF + (t2 + 1) * 512],
                                in0=pb[par][:, :],
                                in1=X[:, ofc * HALF + t2 * 512: ofc * HALF + (t2 + 1) * 512]),
                                inc="sV", waits=ws)
                        xw[ofc] = ("sV", xv)
                        bankB[par].used(("sV", xv))
            blanket = ("sPE", em.cnt["sPE"])
            for ii in range(8):
                wdB[ii].used(blanket)

        do_ffn1 = "ffn1" in stages
        do_attn = "attn" in stages
        do_ffn2 = "ffn2" in stages

        x_load(0, first=True)
        em.op("dve", lambda e: e.tensor_scalar_mul(
            out=dshalf[:, :], in0=ds[:, :], scalar1=fc[:, FC_HALF:FC_HALF + 1]),
            inc="sV", waits=[("sL", em.cnt["sL"])])
        if do_ffn1:
            layernorm()
            ffn()
        if do_attn:
            layernorm()
            x_store(0)
            x_load(1, first=True)
            eg0 = qkv(0, defer_gather=True)
        else:
            x_store(0)
            x_load(1, first=True)
        if do_ffn1:
            layernorm()
            ffn(early_hook=(eg0 if do_attn else None))
        elif do_attn:
            eg0()
        if do_attn:
            layernorm()
            eg1 = qkv(1, defer_gather=True)
            attention(pre_hook=eg1,
                      mid_hook={4: lambda: (x_store(1),
                                            x_load(0, first=False))})
            wo_proj(0)
        else:
            x_store(1)
            x_load(0, first=False)
        if do_ffn2:
            layernorm()
            ffn()
        x_store(0)
        x_load(1, first=False)
        if do_attn:
            wo_proj(1)
        if do_ffn2:
            layernorm()
            ffn()
        x_store(1)
        for sname in semnames:
            em.wait("sync", (sname, em.cnt[sname]))

        n_ops = sum(len(v) for v in em.ops.values())

        block = es.enter_context(nc.Block())
        block.tensor(lambda e: [fn(e) for fn in em.ops["pe"]])
        block.scalar(lambda e: [fn(e) for fn in em.ops["act"]])
        block.vector(lambda e: [fn(e) for fn in em.ops["dve"]])
        block.gpsimd(lambda e: [fn(e) for fn in em.ops["pool"]])
        block.sync(lambda e: [fn(e) for fn in em.ops["sync"]])

    return nc, n_ops


def _prep_maps(inputs):
    gate_w = np.ascontiguousarray(np.asarray(inputs["L0_ffn_gate_w"], np.int32).T)
    up_w = np.ascontiguousarray(np.asarray(inputs["L0_ffn_up_w"], np.int32).T)
    down_w = np.ascontiguousarray(np.asarray(inputs["L0_ffn_down_w"], np.int32).T)
    shared = {
        "wg4": np.ascontiguousarray(gate_w.reshape(16, 128, 64, 128).transpose(2, 1, 0, 3)),
        "wu4": np.ascontiguousarray(up_w.reshape(16, 128, 64, 128).transpose(2, 1, 0, 3)),
        "wd3": np.ascontiguousarray(down_w.reshape(64, 128, 2048)),
        "wqkv4": np.ascontiguousarray(
            np.asarray(inputs["L0_attn_Wqkv"], np.float32).reshape(16, 128, 48, 128).transpose(2, 1, 0, 3)),
        "wo4": np.ascontiguousarray(
            np.asarray(inputs["L0_attn_Wo"], np.float32).reshape(16, 128, 16, 128).transpose(2, 1, 0, 3)),
        "bqkvT": np.ascontiguousarray(np.asarray(inputs["L0_attn_bqkv"], np.float32).reshape(48, 128).T),
        "boT": np.ascontiguousarray(np.asarray(inputs["L0_attn_bo"], np.float32).reshape(16, 128).T),
        "sgT": np.ascontiguousarray(np.asarray(inputs["L0_ffn_gate_scale"], np.float32).reshape(64, 128).T),
        "suT": np.ascontiguousarray(np.asarray(inputs["L0_ffn_up_scale"], np.float32).reshape(64, 128).T),
        "dsT": np.ascontiguousarray(np.asarray(inputs["L0_ffn_down_scale"], np.float32).reshape(16, 128).T),
    }
    x = np.asarray(inputs["L0_attn_x"], np.float32)
    p = np.arange(128, dtype=np.int32)
    in_maps = []
    for c in range(8):
        b, me = c // 2, c % 2
        idx = np.zeros((128, 64), dtype=np.int32)
        for j in range(2):
            for sm in range(8):
                idx[:, j * 8 + sm] = j * 4096 + (8 * me + sm) * 128 + p
                idx[:, 16 + j * 8 + sm] = j * 4096 + 2048 + (8 * me + sm) * 128 + p
            for ckh in range(2):
                for ks in range(4):
                    idx[:, 32 + j * 8 + ckh * 4 + ks] = (
                        j * 2048 + me * 1024 + ckh * 512 + ks * 128 + p)
        for fc in range(16):
            idx[:, 48 + fc] = ((fc // 8) * 2 + me) * 512 + (fc % 8 % 4) * 128 + p
        m = dict(shared)
        m["xin"] = np.ascontiguousarray(x[b].T[:, me * T:(me + 1) * T])
        m["idxd"] = idx
        in_maps.append(m)
    return in_maps


_CACHED = {}


def run_device(inputs, stages=STAGES_FULL):
    key = tuple(stages)
    if key not in _CACHED:
        _CACHED[key] = build_program(stages)
    nc, _ = _CACHED[key]
    in_maps = _prep_maps(inputs)
    res = run_bass_kernel_spmd(nc, in_maps, core_ids=list(range(8)))
    out = np.empty((4, 4096, 2048), dtype=np.float32)
    for c in range(8):
        b, half = c // 2, c % 2
        out[b, half * T:(half + 1) * T, :] = np.asarray(res.results[c]["y"]).T
    return out


def kernel(**inputs):
    return run_device(inputs, STAGES_FULL)


# version 29
# speedup vs baseline: 1.0281x; 1.0281x over previous
import contextlib
import math

import numpy as np
import ml_dtypes

import concourse.bass as bass
from concourse.bass_core_rust import IndirectOffsetOnAxis
import concourse.mybir as mybir
from concourse.bass_utils import run_bass_kernel_spmd

F32 = mybir.dt.float32
BF16 = mybir.dt.bfloat16
I32 = mybir.dt.int32
AF = mybir.ActivationFunctionType
ALU = mybir.AluOpType

D = 2048
NH = 16
DH = 128
T = 2048
HALF = 1024
NFC = 16
NIC = 64
BAND = 8
NB = NIC // BAND
SOFT_SCALE = 1.0 / math.sqrt(128.0)
CSHIFT = -20.0
MASKVAL = -10000.0 * math.sqrt(128.0)

FC_EPS, FC_NHALF, FC_SSCALE, FC_CSHIFT, FC_INV2048, FC_HALF = 0, 1, 2, 3, 4, 5
FC_N = 8

ENGINES = ("pe", "act", "dve", "pool", "sync")
PARANOIA = 0
STAGES_FULL = ("ffn1", "attn", "ffn2")
RG = [[0, 1], [2, 3], [4, 5], [6, 7]]


class Em:
    def __init__(self, sems):
        self.sems = sems
        self.cnt = {k: 0 for k in sems}
        self.ops = {e: [] for e in ENGINES}
        self.floor = {}

    def wait(self, eng, w):
        if w is None:
            return
        sem, val = w
        if val is None or val <= 0:
            return
        assert val <= self.cnt[sem], (
            f"deadlock: {eng} waits {sem}>={val}, emitted {self.cnt[sem]}")
        key = (eng, sem)
        if self.floor.get(key, 0) >= val:
            return
        self.floor[key] = val
        h = self.sems[sem]
        self.ops[eng].append(lambda e, h=h, v=val: e.wait_ge(h, v))

    def waits(self, eng, ws):
        for w in ws:
            self.wait(eng, w)

    SELF_SEM = {"dve": "sV", "act": "sA", "pool": "sP"}

    ENG_INC = {"pe": "sPE", "act": "sA", "dve": "sV", "pool": "sP", "sync": "sP"}

    def op(self, eng, fn, inc=None, inc_n=1, waits=()):
        if PARANOIA == 2:
            for s2 in self.cnt:
                self.wait(eng, (s2, self.cnt[s2]))
            if inc is None:
                inc = self.ENG_INC[eng]
        elif PARANOIA == 3 and eng in ("pe", "act", "dve", "pool"):
            tgt = ("sA", "sV", "sP") if eng == "pe" else ("sPE",)
            for s2 in tgt:
                self.wait(eng, (s2, self.cnt[s2]))
            if inc is None:
                inc = self.ENG_INC[eng]
        elif PARANOIA == 4 and eng in ("pe", "act", "dve", "pool"):
            for s2 in ("sW", "sL", "sS"):
                self.wait(eng, (s2, self.cnt[s2]))
            if inc is None:
                inc = self.ENG_INC[eng]
        elif PARANOIA == 5 and eng in ("pe", "act", "dve", "pool"):
            if inc is None:
                inc = self.ENG_INC[eng]
        elif PARANOIA == 6 and eng in ("pe", "act", "dve", "pool"):
            for s2 in ("sW", "sL", "sS"):
                self.wait(eng, (s2, self.cnt[s2]))
            if inc is None:
                inc = self.ENG_INC[eng]
        ssem = self.SELF_SEM.get(eng)
        if ssem is not None:
            self.wait(eng, (ssem, self.cnt[ssem]))
            if inc is None:
                inc, inc_n = ssem, 1
        self.waits(eng, waits)
        if inc is None:
            self.ops[eng].append(fn)
            return None
        self.cnt[inc] += inc_n
        h = self.sems[inc]
        self.ops[eng].append(lambda e, fn=fn, h=h, n=inc_n: fn(e).then_inc(h, n))
        return self.cnt[inc]

    def dma(self, queue, dst, src, sem, waits=()):
        ws = list(waits)
        if PARANOIA == 2:
            ws += [(s2, self.cnt[s2]) for s2 in self.cnt]
        elif PARANOIA == 4:
            ws += [(s2, self.cnt[s2]) for s2 in ("sPE", "sA", "sV", "sP")]
        elif PARANOIA == 7:
            ws += [(s2, self.cnt[s2]) for s2 in ("sPE", "sA", "sV", "sP")]
        ssem = self.SELF_SEM.get(queue)
        if ssem is not None:
            ws.append((ssem, self.cnt[ssem]))
        self.waits(queue, ws)
        self.cnt[sem] += 16
        h = self.sems[sem]
        self.ops[queue].append(
            lambda e, dst=dst, src=src, h=h: e.dma_start(out=dst, in_=src).then_inc(h, 16))
        return self.cnt[sem]


class Buf:
    def __init__(self):
        self.ready = None
        self.users = []

    def reuse_waits(self):
        ws = list(self.users)
        if self.ready is not None:
            ws.append(self.ready)
        return ws

    def set_ready(self, w):
        self.ready = w
        self.users = []

    def used(self, w):
        if w is not None:
            self.users.append(w)


def build_program(stages=STAGES_FULL):
    nc = bass.Bass("TRN2", target_bir_lowering=False)

    xin = nc.dram_tensor("xin", [D, T], F32, kind="ExternalInput").ap()
    y = nc.dram_tensor("y", [D, T], F32, kind="ExternalOutput").ap()
    wg4 = nc.dram_tensor("wg4", [NIC, 128, NFC, 128], I32, kind="ExternalInput").ap()
    wu4 = nc.dram_tensor("wu4", [NIC, 128, NFC, 128], I32, kind="ExternalInput").ap()
    wd3 = nc.dram_tensor("wd3", [NIC, 128, D], I32, kind="ExternalInput").ap()
    wqkv4 = nc.dram_tensor("wqkv4", [48, 128, NFC, 128], F32, kind="ExternalInput").ap()
    wo4 = nc.dram_tensor("wo4", [NFC, 128, NFC, 128], F32, kind="ExternalInput").ap()
    bqkvT = nc.dram_tensor("bqkvT", [128, 48], F32, kind="ExternalInput").ap()
    boT = nc.dram_tensor("boT", [128, NFC], F32, kind="ExternalInput").ap()
    sgT = nc.dram_tensor("sgT", [128, NIC], F32, kind="ExternalInput").ap()
    suT = nc.dram_tensor("suT", [128, NIC], F32, kind="ExternalInput").ap()
    dsT = nc.dram_tensor("dsT", [128, NFC], F32, kind="ExternalInput").ap()
    idxd = nc.dram_tensor("idxd", [128, 64], I32, kind="ExternalInput").ap()

    kvq_own = [nc.dram_tensor(f"kvq_own{g}", [4096, HALF], BF16) for g in range(2)]
    kvq_g = [nc.dram_tensor(f"kvq_g{g}", [2 * 4096, HALF], BF16) for g in range(2)]
    kvqv_own = [nc.dram_tensor(f"kvqv_own{g}", [2048, HALF], BF16) for g in range(2)]
    kvqv_g = [nc.dram_tensor(f"kvqv_g{g}", [2 * 2048, HALF], BF16) for g in range(2)]
    o_own_a = nc.dram_tensor("o_own_a", [2, 4, 128, T], BF16)
    o_own_b = nc.dram_tensor("o_own_b", [2, 4, 128, T], BF16)
    o_ga = nc.dram_tensor("o_ga", [4, 4, 128, T], BF16)
    o_gb = nc.dram_tensor("o_gb", [4, 4, 128, T], BF16)

    bfdt = ml_dtypes.bfloat16
    ident_c = nc.inline_tensor(np.eye(128, dtype=bfdt), "ident_c")
    selc2_np = np.zeros((128, 4), dtype=bfdt)
    selc2_np[:, 0] = 1.0
    selc2_np[:, 3] = 1.0
    selc2_c = nc.inline_tensor(selc2_np, "selc2_c")
    selc4_np = np.zeros((128, 16), dtype=bfdt)
    for s in range(4):
        selc4_np[:, 4 * s + s] = 1.0
    selc4_c = nc.inline_tensor(selc4_np, "selc4_c")
    selbc_np = np.zeros((4, 512), dtype=bfdt)
    for s in range(4):
        selbc_np[s, 128 * s:128 * (s + 1)] = 1.0
    selbc_c = nc.inline_tensor(selbc_np, "selbc_c")
    selr2_np = np.zeros((2, 256), dtype=bfdt)
    selr2_np[0, 0:128] = 1.0
    selr2_np[1, 128:256] = 1.0
    selr2_c = nc.inline_tensor(selr2_np, "selr2_c")
    negsel_np = np.zeros((2, 256), dtype=bfdt)
    negsel_np[0, 0:128] = -128.0
    negsel_np[1, 128:256] = -128.0
    negsel_c = nc.inline_tensor(negsel_np, "negsel_c")
    fc_np = np.zeros((128, FC_N), dtype=np.float32)
    fc_np[:, FC_EPS] = 1e-5
    fc_np[:, FC_NHALF] = -0.5
    fc_np[:, FC_SSCALE] = SOFT_SCALE
    fc_np[:, FC_CSHIFT] = CSHIFT
    fc_np[:, FC_INV2048] = 1.0 / 2048.0
    fc_np[:, FC_HALF] = 0.5
    fc_c = nc.inline_tensor(fc_np, "fc_c")
    pp = np.arange(128)[:, None]
    qq = np.arange(512)[None, :]
    mk = np.zeros((128, 2048), dtype=np.float32)
    for ks in range(4):
        mk[:, ks * 512:(ks + 1) * 512] = np.where(ks * 128 + pp > qq, MASKVAL, 0.0)
    masks_c = nc.inline_tensor(mk, "masks_c")

    with contextlib.ExitStack() as es:
        sb = lambda n, s, d: es.enter_context(nc.sbuf_tensor(n, s, d))
        ps = lambda n, s, d: es.enter_context(nc.psum_tensor(n, s, d))
        X = sb("X", [128, NFC * HALF], F32)
        xln = sb("xln", [128, NFC * HALF], BF16)
        Hsb = sb("Hsb", [128, 8 * HALF], BF16)
        wgu = sb("wgu", [128, 4 * 2048], BF16)
        wd = sb("wd", [128, 8 * 2048], BF16)
        tmp = sb("tmp", [128, 2 * 512], F32)
        stg = sb("stg", [128, 4 * 512], BF16)
        ptb = sb("ptb", [128, 4 * 512], BF16)
        wos = sb("wos", [128, 2 * 1024], F32)
        smf = [sb(f"smf{i}", [4, 512], F32) for i in range(3)]
        smb = [sb(f"smb{i}", [4, 512], BF16) for i in range(5)]
        ident = sb("ident", [128, 128], BF16)
        selc2 = sb("selc2", [128, 4], BF16)
        selc4 = sb("selc4", [128, 16], BF16)
        selbc = sb("selbc", [4, 512], BF16)
        selr2 = sb("selr2", [2, 256], BF16)
        negsel = sb("negsel", [2, 256], BF16)
        fc = sb("fc", [128, FC_N], F32)
        masks = sb("masks", [128, 2048], F32)
        idxt = sb("idxt", [128, 64], I32)
        bqkv = sb("bqkv", [128, 48], F32)
        bo = sb("bo", [128, NFC], F32)
        sg = sb("sg", [128, NIC], F32)
        su = sb("su", [128, NIC], F32)
        ds = sb("ds", [128, NFC], F32)
        dshalf = sb("dshalf", [128, NFC], F32)

        pb = [ps(f"pb{i}", [128, 512], F32) for i in range(7)]
        pvt = ps("pvt", [128, 1024], BF16)

        semnames = ("sW", "sL", "sS", "sPE", "sA", "sV", "sP",
                    "sSL0", "sSL1", "sSL2", "sSL3", "sWd", "sOq",
                    "sKV0", "sKV1", "sKV2", "sKV3", "sKB0", "sKB1",
                    "sSt0", "sSt1", "sSt2", "sSt3",
                    "sCk0", "sCk1", "sCkV0", "sCkV1", "sCo", "sCo2", "sWo")
        sems = {n: es.enter_context(nc.semaphore(n)) for n in semnames}
        em = Em(sems)

        xw = [None] * NFC
        bankB = [Buf() for _ in range(7)]
        pvtB = [Buf(), Buf()]
        xlnB = Buf()
        hB = [Buf() for _ in range(8)]
        hKB = [Buf(), Buf()]
        wguB = [Buf() for _ in range(4)]
        wdB = [Buf() for _ in range(8)]
        tmpB = [Buf(), Buf()]
        stgB = [Buf() for _ in range(4)]
        ptB = [Buf(), Buf(), Buf(), Buf()]
        smfB = [Buf() for _ in range(3)]
        smbB = [Buf() for _ in range(5)]
        gather_done = {}

        def smf_ap(idx, rows=2):
            return smf[idx][0:rows, :]

        def smb_ap(idx, rows=2):
            return smb[idx][0:rows, :]

        SA, SB_, SC_ = 0, 1, 2
        B_RHI, B_RLO, B_MRHI, B_MRLO, B_XSUM = range(5)

        cv = None
        for dst, src in [
            (ident[:, :], ident_c.ap()), (selc2[:, :], selc2_c.ap()),
            (selc4[:, :], selc4_c.ap()), (selbc[:, :], selbc_c.ap()),
            (selr2[:, :], selr2_c.ap()), (negsel[:, :], negsel_c.ap()),
            (fc[:, :], fc_c.ap()), (masks[:, :], masks_c.ap()),
            (idxt[:, :], idxd),
            (bqkv[:, :], bqkvT), (bo[:, :], boT),
            (sg[:, :], sgT), (su[:, :], suT), (ds[:, :], dsT),
        ]:
            cv = em.dma("sync", dst, src, "sL")
        def x_load(half, first):
            src = xin if first else y
            v = None
            for c in range(NFC):
                ws = [xw[c]] if xw[c] is not None else []
                v = em.dma("sync", X[:, c * HALF:(c + 1) * HALF],
                           src[c * 128:(c + 1) * 128, half * HALF:(half + 1) * HALF],
                           "sL", waits=ws)
            for c in range(NFC):
                xw[c] = ("sL", v)

        def x_store(half):
            v = None
            for c in range(NFC):
                ws = [xw[c]] if xw[c] is not None else []
                v = em.dma("sync", y[c * 128:(c + 1) * 128, half * HALF:(half + 1) * HALF],
                           X[:, c * HALF:(c + 1) * HALF], "sS", waits=ws)
            for c in range(NFC):
                xw[c] = ("sS", v)

        def layernorm():
            stats_v = None
            for c in range(NFC):
                sl_b, sl_q = c % 2, 2 + (c % 2)
                ws = hB[sl_b].reuse_waits() + ([xw[c]] if xw[c] else [])
                vb = em.op("dve", lambda e, sl=sl_b, c=c: e.tensor_copy(
                    out=Hsb[:, sl * HALF:(sl + 1) * HALF],
                    in_=X[:, c * HALF:(c + 1) * HALF]), inc="sV", waits=ws)
                hB[sl_b].set_ready(("sV", vb))
                ws = hB[sl_q].reuse_waits() + ([xw[c]] if xw[c] else [])
                vq = em.op("act", lambda e, sl=sl_q, c=c: e.activation(
                    out=Hsb[:, sl * HALF:(sl + 1) * HALF],
                    in_=X[:, c * HALF:(c + 1) * HALF], func=AF.Square),
                    inc="sA", waits=ws)
                hB[sl_q].set_ready(("sA", vq))
                ws = [("sA", vq), ("sV", vb)]
                if c == 0:
                    ws += bankB[0].reuse_waits() + bankB[1].reuse_waits()
                for t2 in range(2):
                    em.op("pe", lambda e, t2=t2, sl=sl_b, c=c: e.matmul(
                        pb[0][0:2, :], selc2[:, 2 * t2:2 * t2 + 2],
                        Hsb[:, sl * HALF + t2 * 512: sl * HALF + (t2 + 1) * 512],
                        start=(c == 0 and t2 == 0),
                        stop=(c == NFC - 1 and t2 == 1)), waits=ws)
                    ws = []
                    last = (c == NFC - 1 and t2 == 1)
                    v = em.op("pe", lambda e, t2=t2, sl=sl_q, c=c, last=last: e.matmul(
                        pb[1][0:2, :], selc2[:, 2 * t2:2 * t2 + 2],
                        Hsb[:, sl * HALF + t2 * 512: sl * HALF + (t2 + 1) * 512],
                        start=(c == 0 and t2 == 0), stop=last),
                        inc="sPE")
                    stats_v = v
                hB[sl_b].used(("sPE", stats_v))
                hB[sl_q].used(("sPE", stats_v))
            def dve_small(fn, kind, idx, waits):
                tr = smfB if kind == "f" else smbB
                v = em.op("dve", fn, inc="sV", waits=list(waits) + tr[idx].reuse_waits())
                tr[idx].set_ready(("sV", v))
                return v

            v = dve_small(lambda e: e.tensor_copy(out=smf_ap(SA), in_=pb[0][0:2, :]),
                          "f", SA, [("sPE", stats_v)])
            bankB[0].set_ready(None); bankB[0].users = [("sV", v)]
            v = dve_small(lambda e: e.tensor_copy(out=smf_ap(SB_), in_=pb[1][0:2, :]),
                          "f", SB_, [])
            bankB[1].set_ready(None); bankB[1].users = [("sV", v)]
            dve_small(lambda e: e.tensor_scalar_mul(
                out=smf_ap(SC_), in0=smf_ap(SA),
                scalar1=fc[0:2, FC_INV2048:FC_INV2048 + 1]), "f", SC_, [])
            smfB[SA].used(("sV", em.cnt["sV"]))
            dve_small(lambda e: e.tensor_mul(
                out=smf_ap(SA), in0=smf_ap(SC_), in1=smf_ap(SC_)), "f", SA, [])
            var_v = dve_small(lambda e: e.scalar_tensor_tensor(
                out=smf_ap(SB_), in0=smf_ap(SB_),
                scalar=fc[0:2, FC_INV2048:FC_INV2048 + 1],
                in1=smf_ap(SA), op0=ALU.mult, op1=ALU.subtract), "f", SB_, [])
            smfB[SA].used(("sV", var_v))
            v = em.op("act", lambda e: e.activation(
                out=smf_ap(SA), in_=smf_ap(SB_), func=AF.Ln,
                bias=fc[0:2, FC_EPS:FC_EPS + 1]),
                inc="sA", waits=[("sV", var_v)] + smfB[SA].reuse_waits())
            r_v = em.op("act", lambda e: e.activation(
                out=smf_ap(SA), in_=smf_ap(SA), func=AF.Exp,
                scale=fc[0:2, FC_NHALF:FC_NHALF + 1]), inc="sA")
            smfB[SA].set_ready(("sA", r_v))
            smfB[SB_].used(("sA", r_v))
            dve_small(lambda e: e.tensor_mul(
                out=smf_ap(SB_), in0=smf_ap(SC_), in1=smf_ap(SA)), "f", SB_,
                [("sA", r_v)])
            smfB[SC_].used(("sV", em.cnt["sV"]))
            smfB[SA].used(("sV", em.cnt["sV"]))
            split_v = None
            for src_s, hi_b, lo_b in ((SA, B_RHI, B_RLO), (SB_, B_MRHI, B_MRLO)):
                dve_small(lambda e, src_s=src_s, hi_b=hi_b: e.tensor_copy(
                    out=smb_ap(hi_b), in_=smf_ap(src_s)), "b", hi_b, [])
                dve_small(lambda e, hi_b=hi_b: e.tensor_copy(
                    out=smf_ap(SC_), in_=smb_ap(hi_b)), "f", SC_, [])
                split_v = dve_small(lambda e, src_s=src_s, lo_b=lo_b: e.tensor_sub(
                    out=smb_ap(lo_b), in0=smf_ap(src_s), in1=smf_ap(SC_)), "b", lo_b, [])
                smfB[SC_].used(("sV", split_v))
                smfB[src_s].used(("sV", split_v))
            for t2 in range(2):
                ws = ([("sV", split_v)] + bankB[4].reuse_waits()
                      + bankB[5].reuse_waits())
                em.op("pe", lambda e, t2=t2: e.matmul(
                    pb[4][:, :], selr2[:, 128 * t2:128 * (t2 + 1)], smb_ap(B_RHI),
                    start=True, stop=False), waits=ws)
                em.op("pe", lambda e, t2=t2: e.matmul(
                    pb[4][:, :], selr2[:, 128 * t2:128 * (t2 + 1)], smb_ap(B_RLO),
                    start=False, stop=True))
                em.op("pe", lambda e, t2=t2: e.matmul(
                    pb[5][:, :], selr2[:, 128 * t2:128 * (t2 + 1)], smb_ap(B_MRHI),
                    start=True, stop=False))
                bc_v = em.op("pe", lambda e, t2=t2: e.matmul(
                    pb[5][:, :], selr2[:, 128 * t2:128 * (t2 + 1)], smb_ap(B_MRLO),
                    start=False, stop=True), inc="sPE")
                bankB[4].set_ready(("sPE", bc_v))
                bankB[5].set_ready(("sPE", bc_v))
                smbB[B_RHI].used(("sPE", bc_v))
                smbB[B_RLO].used(("sPE", bc_v))
                smbB[B_MRHI].used(("sPE", bc_v))
                smbB[B_MRLO].used(("sPE", bc_v))
                last_tt = None
                for c in range(NFC):
                    ws = [("sPE", bc_v)] + ([xw[c]] if xw[c] else [])
                    if c == 0:
                        ws += xlnB.reuse_waits() + tmpB[0].reuse_waits()
                    em.op("dve", lambda e, c=c, t2=t2: e.tensor_mul(
                        out=tmp[:, 0:512],
                        in0=X[:, c * HALF + t2 * 512: c * HALF + (t2 + 1) * 512],
                        in1=pb[4][:, :]), waits=ws)
                    last_tt = em.op("dve", lambda e, c=c, t2=t2: e.tensor_sub(
                        out=xln[:, c * HALF + t2 * 512: c * HALF + (t2 + 1) * 512],
                        in0=tmp[:, 0:512], in1=pb[5][:, :]), inc="sV")
                bankB[4].used(("sV", last_tt))
                bankB[5].used(("sV", last_tt))
                if t2 == 0:
                    xlnB.set_ready(("sV", last_tt))
                    tmpB[0].set_ready(("sV", last_tt))
                else:
                    xlnB.ready = ("sV", last_tt)
                    tmpB[0].ready = ("sV", last_tt)
            xs_v = None
            ws = bankB[2].reuse_waits() + [xlnB.ready]
            for c in range(NFC):
                for t2 in range(2):
                    last = (c == NFC - 1 and t2 == 1)
                    xs_v = em.op("pe", lambda e, c=c, t2=t2, last=last: e.matmul(
                        pb[2][0:2, :], selc2[:, 2 * t2:2 * t2 + 2],
                        xln[:, c * HALF + t2 * 512: c * HALF + (t2 + 1) * 512],
                        start=(c == 0 and t2 == 0), stop=last),
                        inc=("sPE" if last else None), waits=ws) or xs_v
                    ws = []
            v = em.op("dve", lambda e: e.tensor_copy(out=smb_ap(B_XSUM), in_=pb[2][0:2, :]),
                      inc="sV", waits=[("sPE", xs_v)] + smbB[B_XSUM].reuse_waits())
            smbB[B_XSUM].set_ready(("sV", v))
            bankB[2].set_ready(None); bankB[2].users = [("sV", v)]

        def load_gu(i):
            gslot, uslot = i % 2, 2 + (i % 2)
            for slot, src in ((gslot, wg4), (uslot, wu4)):
                ws = wguB[slot].reuse_waits()
                v = em.dma("pool", wgu[:, slot * 2048:(slot + 1) * 2048],
                           src[i].rearrange("p a b -> p (a b)"), f"sSL{slot}", waits=ws)
                wguB[slot].set_ready((f"sSL{slot}", v))

        def ffn(after_band_hook=None, early_hook=None):
            xln_r = xlnB.ready
            xsum_r = smbB[B_XSUM].ready
            load_gu(0)
            load_gu(1)
            uidx = 0
            for b in range(NB):
                dn_state = {"i": 0, "v": None}

                def emit_down(k):
                    for _ in range(k):
                        ii = dn_state["i"]
                        if ii >= BAND:
                            return
                        dn_state["i"] += 1
                        i = b * BAND + ii
                        ws = wdB[ii].reuse_waits()
                        dn_state["v"] = em.dma(
                            "pool", wd[:, ii * 2048:(ii + 1) * 2048], wd3[i],
                            "sWd", waits=ws)
                if b == 0 and early_hook is not None:
                    emit_down(BAND)
                    early_hook()
                hs_mm_v = None
                hs_pend = []

                def flush_hsum():
                    ws, ii, t2, last = hs_pend.pop(0)
                    return em.op("pe", lambda e, ii=ii, t2=t2, last=last: e.matmul(
                        pb[6][0:2, :], selc2[:, 2 * t2:2 * t2 + 2],
                        Hsb[:, ii * HALF + t2 * 512: ii * HALF + (t2 + 1) * 512],
                        start=(ii == 0 and t2 == 0), stop=last),
                        inc=("sPE" if last else None), waits=ws)

                for ii in range(BAND):
                    i = b * BAND + ii
                    gslot, uslot = i % 2, 2 + (i % 2)
                    for t2 in range(2):
                        par = uidx % 2
                        gb, ub = par, 2 + par
                        ws = [wguB[gslot].ready, xln_r, xsum_r] + bankB[gb].reuse_waits()
                        for hc in range(NFC):
                            em.op("pe", lambda e, gslot=gslot, hc=hc, t2=t2, gb=gb: e.matmul(
                                pb[gb][:, :],
                                wgu[:, gslot * 2048 + hc * 128: gslot * 2048 + (hc + 1) * 128],
                                xln[:, hc * HALF + t2 * 512: hc * HALF + (t2 + 1) * 512],
                                start=(hc == 0), stop=False), waits=ws)
                            ws = []
                        g_v = em.op("pe", lambda e, t2=t2, gb=gb: e.matmul(
                            pb[gb][:, :], negsel[:, 128 * t2:128 * (t2 + 1)],
                            smb_ap(B_XSUM), start=False, stop=True), inc="sPE")
                        bankB[gb].set_ready(("sPE", g_v))
                        wguB[gslot].used(("sPE", g_v))
                        ws = [wguB[uslot].ready] + bankB[ub].reuse_waits()
                        for hc in range(NFC):
                            em.op("pe", lambda e, uslot=uslot, hc=hc, t2=t2, ub=ub: e.matmul(
                                pb[ub][:, :],
                                wgu[:, uslot * 2048 + hc * 128: uslot * 2048 + (hc + 1) * 128],
                                xln[:, hc * HALF + t2 * 512: hc * HALF + (t2 + 1) * 512],
                                start=(hc == 0), stop=False), waits=ws)
                            ws = []
                        u_v = em.op("pe", lambda e, t2=t2, ub=ub: e.matmul(
                            pb[ub][:, :], negsel[:, 128 * t2:128 * (t2 + 1)],
                            smb_ap(B_XSUM), start=False, stop=True), inc="sPE")
                        bankB[ub].set_ready(("sPE", u_v))
                        wguB[uslot].used(("sPE", u_v))
                        tslot = par
                        ws = [("sPE", g_v)] + tmpB[tslot].reuse_waits()
                        silu_v = em.op("act", lambda e, gb=gb, i=i, tslot=tslot: e.activation(
                            out=tmp[:, tslot * 512:(tslot + 1) * 512], in_=pb[gb][:, :],
                            func=AF.Silu, scale=sg[:, i:i + 1]), inc="sA", waits=ws)
                        tmpB[tslot].set_ready(("sA", silu_v))
                        bankB[gb].used(("sA", silu_v))
                        ws = [("sA", silu_v), ("sPE", u_v)] + hB[ii].reuse_waits()
                        h_v = em.op("dve", lambda e, ub=ub, i=i, ii=ii, t2=t2, tslot=tslot:
                                    e.scalar_tensor_tensor(
                                        out=Hsb[:, ii * HALF + t2 * 512: ii * HALF + (t2 + 1) * 512],
                                        in0=pb[ub][:, :], scalar=su[:, i:i + 1],
                                        in1=tmp[:, tslot * 512:(tslot + 1) * 512],
                                        op0=ALU.mult, op1=ALU.mult), inc="sV", waits=ws)
                        if t2 == 0:
                            hB[ii].set_ready(("sV", h_v))
                        else:
                            hB[ii].ready = ("sV", h_v)
                        bankB[ub].used(("sV", h_v))
                        tmpB[tslot].used(("sV", h_v))
                        ws = [("sV", h_v)]
                        if ii == 0 and t2 == 0:
                            ws += bankB[6].reuse_waits()
                        last = (ii == BAND - 1 and t2 == 1)
                        hs_pend.append((ws, ii, t2, last))
                        if len(hs_pend) > 2:
                            hs_mm_v = flush_hsum() or hs_mm_v
                        uidx += 1
                        if t2 == 1 and i + 2 < NIC and (i + 2) // BAND == b:
                            load_gu(i + 2)
                    emit_down(2)
                    if after_band_hook is not None:
                        after_band_hook(b)
                    if ii == BAND - 1 and b + 1 < NB:
                        load_gu(b * BAND + BAND)
                        load_gu(b * BAND + BAND + 1)
                while hs_pend:
                    hs_mm_v = flush_hsum() or hs_mm_v
                emit_down(BAND)
                bankB[6].set_ready(("sPE", hs_mm_v))
                v = em.op("dve", lambda e: e.tensor_copy(out=smf_ap(SA), in_=pb[6][0:2, :]),
                          inc="sV", waits=[("sPE", hs_mm_v)] + smfB[SA].reuse_waits())
                smfB[SA].set_ready(("sV", v))
                bankB[6].used(("sV", v))
                v = em.op("dve", lambda e: e.tensor_copy(out=smb_ap(B_RHI), in_=smf_ap(SA)),
                          inc="sV", waits=smbB[B_RHI].reuse_waits())
                smbB[B_RHI].set_ready(("sV", v))
                v = em.op("dve", lambda e: e.tensor_copy(out=smf_ap(SB_), in_=smb_ap(B_RHI)),
                          inc="sV", waits=smfB[SB_].reuse_waits())
                smfB[SB_].set_ready(("sV", v))
                hs_v = em.op("dve", lambda e: e.tensor_sub(
                    out=smb_ap(B_RLO), in0=smf_ap(SA), in1=smf_ap(SB_)),
                    inc="sV", waits=smbB[B_RLO].reuse_waits())
                smbB[B_RLO].set_ready(("sV", hs_v))
                smfB[SA].used(("sV", hs_v))
                smfB[SB_].used(("sV", hs_v))
                for ii in range(BAND):
                    wdB[ii].set_ready(("sWd", dn_state["v"]))
                vidx = 0
                for hf in range(NFC):
                    for t2 in range(2):
                        yb = 4 + (vidx % 2)
                        ws = bankB[yb].reuse_waits()
                        for ii in range(BAND):
                            ws2 = ws + [wdB[ii].ready, hB[ii].ready]
                            em.op("pe", lambda e, ii=ii, hf=hf, t2=t2, yb=yb: e.matmul(
                                pb[yb][:, :],
                                wd[:, ii * 2048 + hf * 128: ii * 2048 + (hf + 1) * 128],
                                Hsb[:, ii * HALF + t2 * 512: ii * HALF + (t2 + 1) * 512],
                                start=(ii == 0), stop=False), waits=ws2)
                            ws = []
                        em.op("pe", lambda e, t2=t2, yb=yb: e.matmul(
                            pb[yb][:, :], negsel[:, 128 * t2:128 * (t2 + 1)],
                            smb_ap(B_RHI), start=False, stop=False),
                            waits=[("sV", hs_v)])
                        y_v = em.op("pe", lambda e, t2=t2, yb=yb: e.matmul(
                            pb[yb][:, :], negsel[:, 128 * t2:128 * (t2 + 1)],
                            smb_ap(B_RLO), start=False, stop=True), inc="sPE")
                        bankB[yb].set_ready(("sPE", y_v))
                        for ii in range(BAND):
                            wdB[ii].used(("sPE", y_v))
                            hB[ii].used(("sPE", y_v))
                        smbB[B_RHI].used(("sPE", y_v))
                        smbB[B_RLO].used(("sPE", y_v))
                        ws = [("sPE", y_v)] + ([xw[hf]] if xw[hf] else [])
                        xv = em.op("dve", lambda e, hf=hf, t2=t2, yb=yb: e.scalar_tensor_tensor(
                            out=X[:, hf * HALF + t2 * 512: hf * HALF + (t2 + 1) * 512],
                            in0=pb[yb][:, :], scalar=dshalf[:, hf:hf + 1],
                            in1=X[:, hf * HALF + t2 * 512: hf * HALF + (t2 + 1) * 512],
                            op0=ALU.mult, op1=ALU.add), inc="sV", waits=ws)
                        xw[hf] = ("sV", xv)
                        bankB[yb].used(("sV", xv))
                        vidx += 1
            blanket = ("sPE", em.cnt["sPE"])
            xlnB.used(blanket)
            smbB[B_XSUM].used(blanket)

        def qkv(g, defer_gather=False):
            xln_r = xlnB.ready
            order = list(range(16, 32)) + list(range(0, 16)) + list(range(32, 48))
            uidx = 0
            vt_pend = []

            def flush_vt():
                vu, ev_v, st_slot, head, t2 = vt_pend.pop(0)
                ph = vu % 2
                ws = [("sA", ev_v)] + pvtB[ph].reuse_waits()
                tp_v = None
                for j in range(4):
                    tp_v = em.op("pe", lambda e, st_slot=st_slot, j=j, ph=ph: e.transpose(
                        pvt[:, ph * 512 + j * 128: ph * 512 + (j + 1) * 128],
                        stg[:, st_slot * 512 + j * 128: st_slot * 512 + (j + 1) * 128],
                        ident[:, :]),
                        inc=("sPE" if j == 3 else None), waits=ws) or tp_v
                    ws = []
                pvtB[ph].set_ready(("sPE", tp_v))
                stgB[st_slot].used(("sPE", tp_v))
                st2 = (vu + 2) % 4
                ws = [("sPE", tp_v)] + stgB[st2].reuse_waits()
                e2_v = em.op("dve", lambda e, ph=ph, st2=st2: e.tensor_copy(
                    out=stg[:, st2 * 512:(st2 + 1) * 512],
                    in_=pvt[:, ph * 512:(ph + 1) * 512]), inc="sV", waits=ws)
                stgB[st2].set_ready(("sV", e2_v))
                pvtB[ph].used(("sV", e2_v))
                fshard, f0 = head // 8, (head % 8) * 128
                dst = (kvqv_own[g].ap()
                       [fshard * HALF + t2 * 512:
                        fshard * HALF + (t2 + 1) * 512, f0:f0 + 128]
                       .rearrange("(j p) d -> p j d", p=128))
                dv = em.dma("sync", dst, stg[:, st2 * 512:(st2 + 1) * 512],
                            f"sSt{st2}", waits=[("sV", e2_v)])
                stgB[st2].used((f"sSt{st2}", dv))

            for n, oc in enumerate(order):
                slot = n % 4
                ws = wguB[slot].reuse_waits()
                v = em.dma("pool", wgu[:, slot * 2048:(slot + 1) * 2048],
                           wqkv4[oc].rearrange("p a b -> p (a b)"), f"sSL{slot}", waits=ws)
                wguB[slot].set_ready((f"sSL{slot}", v))
                for t2 in range(2):
                    par = uidx % 2
                    ws = [wguB[slot].ready, xln_r] + bankB[par].reuse_waits()
                    mm_v = None
                    for hc in range(NFC):
                        mm_v = em.op("pe", lambda e, slot=slot, hc=hc, t2=t2, par=par: e.matmul(
                            pb[par][:, :],
                            wgu[:, slot * 2048 + hc * 128: slot * 2048 + (hc + 1) * 128],
                            xln[:, hc * HALF + t2 * 512: hc * HALF + (t2 + 1) * 512],
                            start=(hc == 0), stop=(hc == NFC - 1)),
                            inc=("sPE" if hc == NFC - 1 else None), waits=ws) or mm_v
                        ws = []
                    bankB[par].set_ready(("sPE", mm_v))
                    wguB[slot].used(("sPE", mm_v))
                    st_slot = uidx % 4
                    ws = [("sPE", mm_v)] + stgB[st_slot].reuse_waits()
                    ev_v = em.op("act", lambda e, par=par, oc=oc, st_slot=st_slot: e.activation(
                        out=stg[:, st_slot * 512:(st_slot + 1) * 512], in_=pb[par][:, :],
                        func=AF.Identity, bias=bqkv[:, oc:oc + 1]), inc="sA", waits=ws)
                    stgB[st_slot].set_ready(("sA", ev_v))
                    bankB[par].used(("sA", ev_v))
                    which, head = divmod(oc, 16)
                    if which in (0, 1):
                        base = (0 if which == 1 else 2048) + head * 128
                        dv = em.dma("sync",
                                    kvq_own[g].ap()[base:base + 128,
                                                    t2 * 512:(t2 + 1) * 512],
                                    stg[:, st_slot * 512:(st_slot + 1) * 512],
                                    f"sSt{st_slot}", waits=[("sA", ev_v)])
                        stgB[st_slot].used((f"sSt{st_slot}", dv))
                    else:
                        vt_pend.append((uidx, ev_v, st_slot, head, t2))
                        if len(vt_pend) > 1:
                            flush_vt()
                    uidx += 1
                if n == 31:
                    stw_kq = [(f"sSt{j}", em.cnt[f"sSt{j}"]) for j in range(4)]
                if n == 47:
                    while vt_pend:
                        flush_vt()
            stw = [(f"sSt{j}", em.cnt[f"sSt{j}"]) for j in range(4)]

            def emit_gather(queue="pool"):
                cv = em.op(queue, lambda e, g=g: e.collective_compute(
                    "AllGather", ALU.bypass, replica_groups=RG,
                    ins=[kvq_own[g].ap()], outs=[kvq_g[g].ap()]),
                    inc=f"sCk{g}", waits=stw_kq)
                gather_done[("kvq", g)] = (f"sCk{g}", cv)
                cv = em.op(queue, lambda e, g=g: e.collective_compute(
                    "AllGather", ALU.bypass, replica_groups=RG,
                    ins=[kvqv_own[g].ap()], outs=[kvqv_g[g].ap()]),
                    inc=f"sCkV{g}", waits=stw)
                gather_done[("kvqV", g)] = (f"sCkV{g}", cv)
            if not defer_gather:
                emit_gather()
            xlnB.used(("sPE", em.cnt["sPE"]))
            return emit_gather

        def kc_map(kc):
            return ((kc % 4) // 2, kc // 4, (kc % 2) * 512)

        def attention(pre_hook=None, mid_hook=None):
            pvtf = pvt.ap().bitcast(F32)
            lB = pvtB[0]
            pair_order = [(0, 0), (0, 1), (1, 0), (1, 1)]
            pair_order += [(0, qs) for qs in range(2, 8)]
            pair_order += [(1, qs) for qs in range(2, 8)]

            def igather(queue, out_ap, g, col, eoff, sem, waits, vten=False):
                em.waits(queue, waits)
                em.cnt[sem] += 16
                h = em.sems[sem]
                flat = (kvqv_g if vten else kvq_g)[g].ap()
                em.ops[queue].append(
                    lambda e, out_ap=out_ap, flat=flat, col=col, eoff=eoff, h=h:
                    e.indirect_dma_start(
                        out=out_ap, out_offset=None, in_=flat,
                        in_offset=IndirectOffsetOnAxis(ap=idxt[:, col:col + 1], axis=0),
                        element_offset=eoff).then_inc(h, 16))
                return em.cnt[sem]

            def emit_q(gi):
                hg, qs = pair_order[2 * gi]
                gq, jq, _ = kc_map(qs)
                P = gi % 2
                ws = (wguB[2 * P].reuse_waits() + wguB[2 * P + 1].reuse_waits()
                      + [gather_done[("kvq", gq)]])
                qv = None
                for s in range(4):
                    qv = igather("pool",
                                 wgu[:, P * 4096 + s * 1024: P * 4096 + (s + 1) * 1024],
                                 gq, 16 + jq * 8 + hg * 4 + s, 0, f"sSL{2 * P}", ws)
                    ws = []
                wguB[2 * P].set_ready((f"sSL{2 * P}", qv))
                wguB[2 * P + 1].set_ready((f"sSL{2 * P}", qv))

            q_emitted = set()

            def maybe_emit_q(pi):
                gi = pi // 2
                if 2 * gi < len(pair_order) and gi not in q_emitted:
                    g_need = kc_map(pair_order[2 * gi][1])[0]
                    if ("kvq", g_need) in gather_done:
                        q_emitted.add(gi)
                        emit_q(gi)

            for blk in range(2):
                us = []
                for sl in range(4 * blk, 4 * blk + 4):
                    us += hB[sl].reuse_waits()
                hKB[blk].users = us
                hKB[blk].ready = None

            fin_pend = []

            def flush_fin():
                while fin_pend:
                    fin_pend.pop(0)()

            def make_finalize(av_v, hg, qs, pi):
                def fin():
                    ws = [("sPE", av_v)] + smfB[SA].reuse_waits()
                    v = em.op("dve", lambda e: e.tensor_copy(
                        out=smf_ap(SA, rows=4), in_=pvtf[0:4, 0:512]),
                        inc="sV", waits=ws)
                    smfB[SA].set_ready(("sV", v))
                    lB.used(("sV", v))
                    v = em.op("dve", lambda e: e.reciprocal(
                        out=smf_ap(SB_, rows=4), in_=smf_ap(SA, rows=4)),
                        inc="sV", waits=smfB[SB_].reuse_waits())
                    smfB[SB_].set_ready(("sV", v))
                    smfB[SA].used(("sV", v))
                    lb = B_RHI
                    lb_v = em.op("dve", lambda e, lb=lb: e.tensor_copy(
                        out=smb_ap(lb, rows=4), in_=smf_ap(SB_, rows=4)),
                        inc="sV", waits=smbB[lb].reuse_waits())
                    smbB[lb].set_ready(("sV", lb_v))
                    smfB[SB_].used(("sV", lb_v))
                    for s in range(4):
                        bcb = s % 2
                        ws = [("sV", lb_v)] + bankB[bcb].reuse_waits()
                        bc_v = em.op("pe", lambda e, s=s, bcb=bcb, lb=lb: e.matmul(
                            pb[bcb][:, :], selbc[:, 128 * s:128 * (s + 1)],
                            smb_ap(lb, rows=4), start=True, stop=True),
                            inc="sPE", waits=ws)
                        bankB[bcb].set_ready(("sPE", bc_v))
                        smbB[lb].used(("sPE", bc_v))
                        tslot = s % 2
                        ws = [("sPE", bc_v)] + tmpB[tslot].reuse_waits()
                        cp_v = em.op("dve", lambda e, bcb=bcb, tslot=tslot: e.tensor_copy(
                            out=tmp[:, tslot * 512:(tslot + 1) * 512], in_=pb[bcb][:, :]),
                            inc="sV", waits=ws)
                        tmpB[tslot].set_ready(("sV", cp_v))
                        bankB[bcb].used(("sV", cp_v))
                        st_slot = s
                        ob = 3 + s
                        ws = stgB[st_slot].reuse_waits()
                        nm_v = em.op("dve", lambda e, ob=ob, tslot=tslot, st_slot=st_slot: e.tensor_mul(
                            out=stg[:, st_slot * 512:(st_slot + 1) * 512],
                            in0=pb[ob][:, :], in1=tmp[:, tslot * 512:(tslot + 1) * 512]),
                            inc="sV", waits=ws)
                        stgB[st_slot].set_ready(("sV", nm_v))
                        tmpB[tslot].used(("sV", nm_v))
                        bankB[ob].used(("sV", nm_v))
                        oo = o_own_a if hg == 0 else o_own_b
                        dv = em.dma("sync",
                                    oo.ap()[qs // 4, s][:, (qs % 4) * 512:(qs % 4) * 512 + 512],
                                    stg[:, st_slot * 512:(st_slot + 1) * 512],
                                    f"sSt{st_slot}", waits=[("sV", nm_v)])
                        stgB[st_slot].used((f"sSt{st_slot}", dv))
                    if pi == 9:
                        stw = [(f"sSt{j}", em.cnt[f"sSt{j}"]) for j in range(4)]
                        cv = em.op("pool", lambda e: e.collective_compute(
                            "AllGather", ALU.bypass, replica_groups=RG,
                            ins=[o_own_a.ap()], outs=[o_ga.ap()]),
                            inc="sCo", waits=stw)
                        gather_done[("o", "a")] = ("sCo", cv)
                return fin

            cur_kb = {}
            cur_v = {}

            def load_kblock(kb, hgl, gk, jk):
                if cur_kb.get(kb) == (hgl, gk, jk):
                    return
                ws = hKB[kb].reuse_waits() + [gather_done[("kvq", gk)]]
                kkv = None
                for s in range(4):
                    kkv = igather(
                        "pool",
                        Hsb[:, kb * 4096 + s * 1024: kb * 4096 + (s + 1) * 1024],
                        gk, jk * 8 + hgl * 4 + s, 0, f"sKB{kb}", ws)
                    ws = []
                hKB[kb].set_ready((f"sKB{kb}", kkv))
                cur_kb[kb] = (hgl, gk, jk)

            def load_v(r, hgl, gk, jk, ckh):
                if cur_v.get(r) == (hgl, gk, jk, ckh):
                    return
                vsl = (2 * r, 2 * r + 1)
                ws = (wdB[vsl[0]].reuse_waits() + wdB[vsl[1]].reuse_waits()
                      + [gather_done[("kvqV", gk)]])
                kv = None
                for ks in range(4):
                    kv = igather("pool",
                                 wd[:, r * 2048 + ks * 512: r * 2048 + (ks + 1) * 512],
                                 gk, 32 + jk * 8 + ckh * 4 + ks,
                                 hgl * 512, f"sKV{r}", ws, vten=True)
                    ws = []
                for sl in vsl:
                    wdB[sl].set_ready((f"sKV{r}", kv))
                cur_v[r] = (hgl, gk, jk, ckh)

            for hgE in (0, 1):
                load_kblock(hgE, hgE, 0, 0)
                maybe_emit_q(2 * hgE)
                for ckh in (0, 1):
                    load_v(2 * hgE + ckh, hgE, 0, 0, ckh)
            if pre_hook is not None:
                pre_hook()
            for pi, (hg, qs) in enumerate(pair_order):
                if mid_hook is not None and pi in mid_hook:
                    mid_hook[pi]()
                maybe_emit_q(pi)
                P2 = (pi // 2) % 2
                cq2 = (qs % 2) * 512
                tile = 0
                nkc = qs + 1
                avl_pend = []
                lv_last = None

                def flush_avl():
                    nonlocal lv_last
                    (ex_v, r, ks, s, ptp, first, last) = avl_pend.pop(0)
                    ob = 3 + s
                    ws = [("sA", ex_v)]
                    if first:
                        ws = ws + bankB[ob].reuse_waits()
                        if s == 0:
                            ws = ws + lB.reuse_waits()
                    em.op("pe", lambda e, r=r, ks=ks, s=s, ptp=ptp, first=first, last=last: e.matmul(
                        pb[3 + s][:, :],
                        wd[:, r * 2048 + ks * 512 + s * 128: r * 2048 + ks * 512 + (s + 1) * 128],
                        ptb[:, ptp * 512:(ptp + 1) * 512],
                        start=first, stop=last), waits=ws)
                    l_v = em.op("pe", lambda e, s=s, ptp=ptp, first=first, last=last: e.matmul(
                        pvtf[0:4, 0:512], selc4[:, 4 * s:4 * s + 4],
                        ptb[:, ptp * 512:(ptp + 1) * 512],
                        start=(first and s == 0), stop=(last and s == 3)),
                        inc="sPE")
                    ptB[ptp].used(("sPE", l_v))
                    wdB[2 * r].used(("sPE", l_v))
                    wdB[2 * r + 1].used(("sPE", l_v))
                    if first:
                        bankB[ob].set_ready(("sPE", l_v))
                        if s == 0:
                            lB.set_ready(("sPE", l_v))
                    lv_last = l_v

                for kc in range(nkc):
                    gk, jk, ck = kc_map(kc)
                    if pi < 4:
                        kb = hg
                        r = 2 * hg + (kc % 2)
                    else:
                        kb = (kc // 2) % 2
                        r = kc % 4
                        load_kblock(kb, hg, gk, jk)
                        load_v(r, hg, gk, jk, ck // 512)
                    vslots = (2 * r, 2 * r + 1)
                    if kc == min(1, nkc - 1):
                        maybe_emit_q(pi + 1)
                    diag = (kc == qs)
                    for ks in range(4):
                        for s in range(4):
                            if tile == 3:
                                flush_fin()
                            par = tile % 3
                            ws = ([hKB[kb].ready, wguB[2 * P2].ready, wdB[vslots[0]].ready]
                                  + bankB[par].reuse_waits())
                            smm_v = em.op("pe", lambda e, kb=kb, ck=ck, s=s, ks=ks,
                                          P2=P2, cq2=cq2, par=par: e.matmul(
                                pb[par][:, :],
                                Hsb[:, kb * 4096 + s * 1024 + ck + ks * 128:
                                    kb * 4096 + s * 1024 + ck + (ks + 1) * 128],
                                wgu[:, P2 * 4096 + s * 1024 + cq2:
                                    P2 * 4096 + s * 1024 + cq2 + 512],
                                start=True, stop=True), inc="sPE", waits=ws)
                            bankB[par].set_ready(("sPE", smm_v))
                            exp_w = [("sPE", smm_v)]
                            if diag:
                                mk_v = em.op("dve", lambda e, par=par, ks=ks: e.tensor_add(
                                    out=pb[par][:, :], in0=pb[par][:, :],
                                    in1=masks[:, ks * 512:(ks + 1) * 512]),
                                    inc="sV", waits=[("sPE", smm_v)])
                                exp_w = [("sV", mk_v)]
                            ptp = tile % 4
                            ws = exp_w + ptB[ptp].reuse_waits()
                            ex_v = em.op("act", lambda e, par=par, ptp=ptp: e.activation(
                                out=ptb[:, ptp * 512:(ptp + 1) * 512], in_=pb[par][:, :],
                                func=AF.Exp, scale=fc[:, FC_SSCALE:FC_SSCALE + 1],
                                bias=fc[:, FC_CSHIFT:FC_CSHIFT + 1]),
                                inc="sA", waits=ws)
                            ptB[ptp].set_ready(("sA", ex_v))
                            bankB[par].used(("sA", ex_v))
                            first = (kc == 0 and ks == 0)
                            last = (kc == nkc - 1 and ks == 3)
                            avl_pend.append((ex_v, r, ks, s, ptp, first, last))
                            if len(avl_pend) > 3:
                                flush_avl()
                            tile += 1
                    hold = ("sPE", em.cnt["sPE"])
                    hKB[kb].used(hold)
                    for sl in vslots:
                        wdB[sl].used(hold)
                while avl_pend:
                    flush_avl()
                hold = ("sPE", em.cnt["sPE"])
                wguB[2 * P2].used(hold)
                wguB[2 * P2 + 1].used(hold)
                fin_pend.append(make_finalize(lv_last, hg, qs, pi))
            flush_fin()
            blanket = ("sPE", em.cnt["sPE"])
            for sl in range(8):
                hB[sl].used(blanket)
                wdB[sl].used(blanket)
            wo_oq_loads(0, True)
            stw = [(f"sSt{j}", em.cnt[f"sSt{j}"]) for j in range(4)]
            cv = em.op("pool", lambda e: e.collective_compute(
                "AllGather", ALU.bypass, replica_groups=RG,
                ins=[o_own_b.ap()], outs=[o_gb.ap()]),
                inc="sCo2", waits=stw)
            gather_done[("o", "b")] = ("sCo2", cv)

        oq_B = [Buf() for _ in range(32)]

        oq_loaded = {}
        wosB = [Buf(), Buf()]

        def wo_oq_loads(half, part_a):
            if (half, part_a) in oq_loaded:
                return
            oq_loaded[(half, part_a)] = True
            if half not in oq_loaded:
                oq_loaded[half] = True
                for slot in range(32):
                    b2 = Buf()
                    b2.users = wdB[slot // 4].reuse_waits() + oq_B[slot].reuse_waits()
                    oq_B[slot] = b2
            v = None
            for qli in range(2):
                ql = half * 2 + qli
                for fcx in range(NFC):
                    if ((fcx % 8) < 4) != part_a:
                        continue
                    slot = qli * 16 + fcx
                    og = o_ga if part_a else o_gb
                    gw = gather_done[("o", "a" if part_a else "b")]
                    ws = oq_B[slot].reuse_waits() + [gw]
                    em.waits("pool", ws)
                    em.cnt["sOq"] += 16
                    h = em.sems["sOq"]
                    flat = og.ap().rearrange("a b p t -> (a b p) t")
                    em.ops["pool"].append(
                        lambda e, slot=slot, flat=flat, fcx=fcx, ql=ql, h=h:
                        e.indirect_dma_start(
                            out=wd[:, slot * 512:(slot + 1) * 512], out_offset=None,
                            in_=flat,
                            in_offset=IndirectOffsetOnAxis(ap=idxt[:, 48 + fcx:49 + fcx], axis=0),
                            element_offset=ql * 512).then_inc(h, 16))
                    v = em.cnt["sOq"]
            for qli in range(2):
                for fcx in range(NFC):
                    if ((fcx % 8) < 4) == part_a:
                        oq_B[qli * 16 + fcx].set_ready(("sOq", v))

        def wo_proj(half):
            wo_oq_loads(half, True)
            wo_oq_loads(half, False)
            passes = ([[f for f in range(NFC) if (f % 8) < 4],
                       [f for f in range(NFC) if (f % 8) >= 4]]
                      if half == 0 else [list(range(NFC))])
            for pn, fcx_list in enumerate(passes):
                stage1 = (half == 0 and pn == 0)
                for ofc in range(NFC):
                    wslot = 2 + (ofc % 2)
                    if stage1:
                        st = ofc % 2
                        ws = wosB[st].reuse_waits()
                        v = None
                        for k, f0 in enumerate((0, 8)):
                            v = em.dma("sync", wos[:, st * 1024 + k * 512: st * 1024 + (k + 1) * 512],
                                       wo4[ofc][:, f0:f0 + 4, :].rearrange("p a b -> p (a b)"),
                                       "sWo", waits=ws)
                            ws = []
                        wosB[st].set_ready(("sWo", v))
                        ws = [wosB[st].ready] + wguB[wslot].reuse_waits()
                        cv = em.op("dve", lambda e, st=st, wslot=wslot: e.tensor_copy(
                            out=wgu[:, wslot * 2048: wslot * 2048 + 1024],
                            in_=wos[:, st * 1024:(st + 1) * 1024]),
                            inc="sV", waits=ws)
                        wguB[wslot].set_ready(("sV", cv))
                        wosB[st].used(("sV", cv))
                    else:
                        ws = wguB[wslot].reuse_waits()
                        v = em.dma("pool", wgu[:, wslot * 2048:(wslot + 1) * 2048],
                                   wo4[ofc].rearrange("p a b -> p (a b)"), f"sSL{wslot}", waits=ws)
                        wguB[wslot].set_ready((f"sSL{wslot}", v))
                    for qli in range(2):
                        t2 = qli
                        par = (ofc * 2 + qli) % 2
                        ws = [wguB[wslot].ready] + bankB[par].reuse_waits()
                        mm_v = None
                        nlast = len(fcx_list) - 1
                        for nf, fcx in enumerate(fcx_list):
                            slot = qli * 16 + fcx
                            wcol = (wslot * 2048 + nf * 128 if stage1
                                    else wslot * 2048 + fcx * 128)
                            ws2 = ws + [oq_B[slot].ready]
                            mm_v = em.op("pe", lambda e, wcol=wcol, slot=slot, par=par,
                                         nf=nf, nlast=nlast: e.matmul(
                                pb[par][:, :],
                                wgu[:, wcol: wcol + 128],
                                wd[:, slot * 512:(slot + 1) * 512],
                                start=(nf == 0), stop=(nf == nlast)),
                                inc=("sPE" if nf == nlast else None), waits=ws2) or mm_v
                            ws = []
                        bankB[par].set_ready(("sPE", mm_v))
                        wguB[wslot].used(("sPE", mm_v))
                        for fcx in fcx_list:
                            oq_B[qli * 16 + fcx].used(("sPE", mm_v))
                        ws = [("sPE", mm_v)] + ([xw[ofc]] if xw[ofc] else [])
                        if pn == 0:
                            xv = em.op("dve", lambda e, ofc=ofc, t2=t2, par=par: e.scalar_tensor_tensor(
                                out=X[:, ofc * HALF + t2 * 512: ofc * HALF + (t2 + 1) * 512],
                                in0=pb[par][:, :], scalar=bo[:, ofc:ofc + 1],
                                in1=X[:, ofc * HALF + t2 * 512: ofc * HALF + (t2 + 1) * 512],
                                op0=ALU.add, op1=ALU.add), inc="sV", waits=ws)
                        else:
                            xv = em.op("dve", lambda e, ofc=ofc, t2=t2, par=par: e.tensor_add(
                                out=X[:, ofc * HALF + t2 * 512: ofc * HALF + (t2 + 1) * 512],
                                in0=pb[par][:, :],
                                in1=X[:, ofc * HALF + t2 * 512: ofc * HALF + (t2 + 1) * 512]),
                                inc="sV", waits=ws)
                        xw[ofc] = ("sV", xv)
                        bankB[par].used(("sV", xv))
            blanket = ("sPE", em.cnt["sPE"])
            for ii in range(8):
                wdB[ii].used(blanket)

        do_ffn1 = "ffn1" in stages
        do_attn = "attn" in stages
        do_ffn2 = "ffn2" in stages

        x_load(0, first=True)
        em.op("dve", lambda e: e.tensor_scalar_mul(
            out=dshalf[:, :], in0=ds[:, :], scalar1=fc[:, FC_HALF:FC_HALF + 1]),
            inc="sV", waits=[("sL", em.cnt["sL"])])
        if do_ffn1:
            layernorm()
            ffn()
        if do_attn:
            layernorm()
            x_store(0)
            x_load(1, first=True)
            eg0 = qkv(0, defer_gather=True)
        else:
            x_store(0)
            x_load(1, first=True)
        if do_ffn1:
            layernorm()
            ffn(early_hook=(eg0 if do_attn else None))
        elif do_attn:
            eg0()
        if do_attn:
            layernorm()
            eg1 = qkv(1, defer_gather=True)
            attention(pre_hook=eg1,
                      mid_hook={4: lambda: (x_store(1),
                                            x_load(0, first=False))})
            wo_proj(0)
        else:
            x_store(1)
            x_load(0, first=False)
        if do_ffn2:
            layernorm()
            ffn()
        x_store(0)
        x_load(1, first=False)
        if do_attn:
            wo_proj(1)
        if do_ffn2:
            layernorm()
            ffn()
        x_store(1)
        for sname in semnames:
            em.wait("sync", (sname, em.cnt[sname]))

        n_ops = sum(len(v) for v in em.ops.values())

        block = es.enter_context(nc.Block())
        block.tensor(lambda e: [fn(e) for fn in em.ops["pe"]])
        block.scalar(lambda e: [fn(e) for fn in em.ops["act"]])
        block.vector(lambda e: [fn(e) for fn in em.ops["dve"]])
        block.gpsimd(lambda e: [fn(e) for fn in em.ops["pool"]])
        block.sync(lambda e: [fn(e) for fn in em.ops["sync"]])

    return nc, n_ops


def _prep_maps(inputs):
    gate_w = np.ascontiguousarray(np.asarray(inputs["L0_ffn_gate_w"], np.int32).T)
    up_w = np.ascontiguousarray(np.asarray(inputs["L0_ffn_up_w"], np.int32).T)
    down_w = np.ascontiguousarray(np.asarray(inputs["L0_ffn_down_w"], np.int32).T)
    shared = {
        "wg4": np.ascontiguousarray(gate_w.reshape(16, 128, 64, 128).transpose(2, 1, 0, 3)),
        "wu4": np.ascontiguousarray(up_w.reshape(16, 128, 64, 128).transpose(2, 1, 0, 3)),
        "wd3": np.ascontiguousarray(down_w.reshape(64, 128, 2048)),
        "wqkv4": np.ascontiguousarray(
            np.asarray(inputs["L0_attn_Wqkv"], np.float32).reshape(16, 128, 48, 128).transpose(2, 1, 0, 3)),
        "wo4": np.ascontiguousarray(
            np.asarray(inputs["L0_attn_Wo"], np.float32).reshape(16, 128, 16, 128).transpose(2, 1, 0, 3)),
        "bqkvT": np.ascontiguousarray(np.asarray(inputs["L0_attn_bqkv"], np.float32).reshape(48, 128).T),
        "boT": np.ascontiguousarray(np.asarray(inputs["L0_attn_bo"], np.float32).reshape(16, 128).T),
        "sgT": np.ascontiguousarray(np.asarray(inputs["L0_ffn_gate_scale"], np.float32).reshape(64, 128).T),
        "suT": np.ascontiguousarray(np.asarray(inputs["L0_ffn_up_scale"], np.float32).reshape(64, 128).T),
        "dsT": np.ascontiguousarray(np.asarray(inputs["L0_ffn_down_scale"], np.float32).reshape(16, 128).T),
    }
    x = np.asarray(inputs["L0_attn_x"], np.float32)
    p = np.arange(128, dtype=np.int32)
    in_maps = []
    for c in range(8):
        b, me = c // 2, c % 2
        idx = np.zeros((128, 64), dtype=np.int32)
        for j in range(2):
            for sm in range(8):
                idx[:, j * 8 + sm] = j * 4096 + (8 * me + sm) * 128 + p
                idx[:, 16 + j * 8 + sm] = j * 4096 + 2048 + (8 * me + sm) * 128 + p
            for ckh in range(2):
                for ks in range(4):
                    idx[:, 32 + j * 8 + ckh * 4 + ks] = (
                        j * 2048 + me * 1024 + ckh * 512 + ks * 128 + p)
        for fc in range(16):
            idx[:, 48 + fc] = ((fc // 8) * 2 + me) * 512 + (fc % 8 % 4) * 128 + p
        m = dict(shared)
        m["xin"] = np.ascontiguousarray(x[b].T[:, me * T:(me + 1) * T])
        m["idxd"] = idx
        in_maps.append(m)
    return in_maps


_CACHED = {}


def run_device(inputs, stages=STAGES_FULL):
    key = tuple(stages)
    if key not in _CACHED:
        _CACHED[key] = build_program(stages)
    nc, _ = _CACHED[key]
    in_maps = _prep_maps(inputs)
    res = run_bass_kernel_spmd(nc, in_maps, core_ids=list(range(8)))
    out = np.empty((4, 4096, 2048), dtype=np.float32)
    for c in range(8):
        b, half = c // 2, c % 2
        out[b, half * T:(half + 1) * T, :] = np.asarray(res.results[c]["y"]).T
    return out


def kernel(**inputs):
    return run_device(inputs, STAGES_FULL)


# version 30
# speedup vs baseline: 1.0703x; 1.0411x over previous
import contextlib
import math

import numpy as np
import ml_dtypes

import concourse.bass as bass
from concourse.bass_core_rust import IndirectOffsetOnAxis
import concourse.mybir as mybir
from concourse.bass_utils import run_bass_kernel_spmd

F32 = mybir.dt.float32
BF16 = mybir.dt.bfloat16
I32 = mybir.dt.int32
AF = mybir.ActivationFunctionType
ALU = mybir.AluOpType

D = 2048
NH = 16
DH = 128
T = 2048
HALF = 1024
NFC = 16
NIC = 64
BAND = 8
NB = NIC // BAND
SOFT_SCALE = 1.0 / math.sqrt(128.0)
CSHIFT = -20.0
MASKVAL = -10000.0 * math.sqrt(128.0)

FC_EPS, FC_NHALF, FC_SSCALE, FC_CSHIFT, FC_INV2048, FC_HALF = 0, 1, 2, 3, 4, 5
FC_N = 8

ENGINES = ("pe", "act", "dve", "pool", "sync")
PARANOIA = 0
STAGES_FULL = ("ffn1", "attn", "ffn2")
RG = [[0, 1], [2, 3], [4, 5], [6, 7]]


class Em:
    def __init__(self, sems):
        self.sems = sems
        self.cnt = {k: 0 for k in sems}
        self.ops = {e: [] for e in ENGINES}
        self.floor = {}

    def wait(self, eng, w):
        if w is None:
            return
        sem, val = w
        if val is None or val <= 0:
            return
        assert val <= self.cnt[sem], (
            f"deadlock: {eng} waits {sem}>={val}, emitted {self.cnt[sem]}")
        key = (eng, sem)
        if self.floor.get(key, 0) >= val:
            return
        self.floor[key] = val
        h = self.sems[sem]
        self.ops[eng].append(lambda e, h=h, v=val: e.wait_ge(h, v))

    def waits(self, eng, ws):
        for w in ws:
            self.wait(eng, w)

    SELF_SEM = {"dve": "sV", "act": "sA", "pool": "sP"}

    ENG_INC = {"pe": "sPE", "act": "sA", "dve": "sV", "pool": "sP", "sync": "sP"}

    def op(self, eng, fn, inc=None, inc_n=1, waits=()):
        if PARANOIA == 2:
            for s2 in self.cnt:
                self.wait(eng, (s2, self.cnt[s2]))
            if inc is None:
                inc = self.ENG_INC[eng]
        elif PARANOIA == 3 and eng in ("pe", "act", "dve", "pool"):
            tgt = ("sA", "sV", "sP") if eng == "pe" else ("sPE",)
            for s2 in tgt:
                self.wait(eng, (s2, self.cnt[s2]))
            if inc is None:
                inc = self.ENG_INC[eng]
        elif PARANOIA == 4 and eng in ("pe", "act", "dve", "pool"):
            for s2 in ("sW", "sL", "sS"):
                self.wait(eng, (s2, self.cnt[s2]))
            if inc is None:
                inc = self.ENG_INC[eng]
        elif PARANOIA == 5 and eng in ("pe", "act", "dve", "pool"):
            if inc is None:
                inc = self.ENG_INC[eng]
        elif PARANOIA == 6 and eng in ("pe", "act", "dve", "pool"):
            for s2 in ("sW", "sL", "sS"):
                self.wait(eng, (s2, self.cnt[s2]))
            if inc is None:
                inc = self.ENG_INC[eng]
        ssem = self.SELF_SEM.get(eng)
        if ssem is not None:
            self.wait(eng, (ssem, self.cnt[ssem]))
            if inc is None:
                inc, inc_n = ssem, 1
        self.waits(eng, waits)
        if inc is None:
            self.ops[eng].append(fn)
            return None
        self.cnt[inc] += inc_n
        h = self.sems[inc]
        self.ops[eng].append(lambda e, fn=fn, h=h, n=inc_n: fn(e).then_inc(h, n))
        return self.cnt[inc]

    def dma(self, queue, dst, src, sem, waits=()):
        ws = list(waits)
        if PARANOIA == 2:
            ws += [(s2, self.cnt[s2]) for s2 in self.cnt]
        elif PARANOIA == 4:
            ws += [(s2, self.cnt[s2]) for s2 in ("sPE", "sA", "sV", "sP")]
        elif PARANOIA == 7:
            ws += [(s2, self.cnt[s2]) for s2 in ("sPE", "sA", "sV", "sP")]
        ssem = self.SELF_SEM.get(queue)
        if ssem is not None:
            ws.append((ssem, self.cnt[ssem]))
        self.waits(queue, ws)
        self.cnt[sem] += 16
        h = self.sems[sem]
        self.ops[queue].append(
            lambda e, dst=dst, src=src, h=h: e.dma_start(out=dst, in_=src).then_inc(h, 16))
        return self.cnt[sem]


class Buf:
    def __init__(self):
        self.ready = None
        self.users = []

    def reuse_waits(self):
        ws = list(self.users)
        if self.ready is not None:
            ws.append(self.ready)
        return ws

    def set_ready(self, w):
        self.ready = w
        self.users = []

    def used(self, w):
        if w is not None:
            self.users.append(w)


def build_program(stages=STAGES_FULL):
    nc = bass.Bass("TRN2", target_bir_lowering=False)

    xin = nc.dram_tensor("xin", [D, T], F32, kind="ExternalInput").ap()
    y = nc.dram_tensor("y", [D, T], F32, kind="ExternalOutput").ap()
    wg4 = nc.dram_tensor("wg4", [NIC, 128, NFC, 128], I32, kind="ExternalInput").ap()
    wu4 = nc.dram_tensor("wu4", [NIC, 128, NFC, 128], I32, kind="ExternalInput").ap()
    wd3 = nc.dram_tensor("wd3", [NIC, 128, D], I32, kind="ExternalInput").ap()
    wqkv4 = nc.dram_tensor("wqkv4", [48, 128, NFC, 128], F32, kind="ExternalInput").ap()
    wo4 = nc.dram_tensor("wo4", [NFC, 128, NFC, 128], F32, kind="ExternalInput").ap()
    bqkvT = nc.dram_tensor("bqkvT", [128, 48], F32, kind="ExternalInput").ap()
    boT = nc.dram_tensor("boT", [128, NFC], F32, kind="ExternalInput").ap()
    sgT = nc.dram_tensor("sgT", [128, NIC], F32, kind="ExternalInput").ap()
    suT = nc.dram_tensor("suT", [128, NIC], F32, kind="ExternalInput").ap()
    dsT = nc.dram_tensor("dsT", [128, NFC], F32, kind="ExternalInput").ap()
    idxd = nc.dram_tensor("idxd", [128, 64], I32, kind="ExternalInput").ap()

    kvq_own = [nc.dram_tensor(f"kvq_own{g}", [4096, HALF], BF16) for g in range(2)]
    kvq_g = [nc.dram_tensor(f"kvq_g{g}", [2 * 4096, HALF], BF16) for g in range(2)]
    kvqv_own = [nc.dram_tensor(f"kvqv_own{g}", [2048, HALF], BF16) for g in range(2)]
    kvqv_g = [nc.dram_tensor(f"kvqv_g{g}", [2 * 2048, HALF], BF16) for g in range(2)]
    o_own_a = nc.dram_tensor("o_own_a", [2, 4, 128, T], BF16)
    o_own_b = nc.dram_tensor("o_own_b", [2, 4, 128, T], BF16)
    o_ga = nc.dram_tensor("o_ga", [4, 4, 128, T], BF16)
    o_gb = nc.dram_tensor("o_gb", [4, 4, 128, T], BF16)

    bfdt = ml_dtypes.bfloat16
    ident_c = nc.inline_tensor(np.eye(128, dtype=bfdt), "ident_c")
    selc2_np = np.zeros((128, 4), dtype=bfdt)
    selc2_np[:, 0] = 1.0
    selc2_np[:, 3] = 1.0
    selc2_c = nc.inline_tensor(selc2_np, "selc2_c")
    selc4_np = np.zeros((128, 16), dtype=bfdt)
    for s in range(4):
        selc4_np[:, 4 * s + s] = 1.0
    selc4_c = nc.inline_tensor(selc4_np, "selc4_c")
    selbc_np = np.zeros((4, 512), dtype=bfdt)
    for s in range(4):
        selbc_np[s, 128 * s:128 * (s + 1)] = 1.0
    selbc_c = nc.inline_tensor(selbc_np, "selbc_c")
    selr2_np = np.zeros((2, 256), dtype=bfdt)
    selr2_np[0, 0:128] = 1.0
    selr2_np[1, 128:256] = 1.0
    selr2_c = nc.inline_tensor(selr2_np, "selr2_c")
    negsel_np = np.zeros((2, 256), dtype=bfdt)
    negsel_np[0, 0:128] = -128.0
    negsel_np[1, 128:256] = -128.0
    negsel_c = nc.inline_tensor(negsel_np, "negsel_c")
    fc_np = np.zeros((128, FC_N), dtype=np.float32)
    fc_np[:, FC_EPS] = 1e-5
    fc_np[:, FC_NHALF] = -0.5
    fc_np[:, FC_SSCALE] = SOFT_SCALE
    fc_np[:, FC_CSHIFT] = CSHIFT
    fc_np[:, FC_INV2048] = 1.0 / 2048.0
    fc_np[:, FC_HALF] = 0.5
    fc_c = nc.inline_tensor(fc_np, "fc_c")
    pp = np.arange(128)[:, None]
    qq = np.arange(512)[None, :]
    mk = np.zeros((128, 2048), dtype=np.float32)
    for ks in range(4):
        mk[:, ks * 512:(ks + 1) * 512] = np.where(ks * 128 + pp > qq, MASKVAL, 0.0)
    masks_c = nc.inline_tensor(mk, "masks_c")

    with contextlib.ExitStack() as es:
        sb = lambda n, s, d: es.enter_context(nc.sbuf_tensor(n, s, d))
        ps = lambda n, s, d: es.enter_context(nc.psum_tensor(n, s, d))
        X = sb("X", [128, NFC * HALF], F32)
        xln = sb("xln", [128, NFC * HALF], BF16)
        Hsb = sb("Hsb", [128, 8 * HALF], BF16)
        wgu = sb("wgu", [128, 4 * 2048], BF16)
        wd = sb("wd", [128, 8 * 2048], BF16)
        tmp = sb("tmp", [128, 2 * 512], F32)
        stg = sb("stg", [128, 4 * 512], BF16)
        ptb = sb("ptb", [128, 4 * 512], BF16)
        wos = sb("wos", [128, 2 * 1024], F32)
        smf = [sb(f"smf{i}", [4, 512], F32) for i in range(3)]
        smb = [sb(f"smb{i}", [4, 512], BF16) for i in range(5)]
        ident = sb("ident", [128, 128], BF16)
        selc2 = sb("selc2", [128, 4], BF16)
        selc4 = sb("selc4", [128, 16], BF16)
        selbc = sb("selbc", [4, 512], BF16)
        selr2 = sb("selr2", [2, 256], BF16)
        negsel = sb("negsel", [2, 256], BF16)
        fc = sb("fc", [128, FC_N], F32)
        masks = sb("masks", [128, 2048], F32)
        idxt = sb("idxt", [128, 64], I32)
        bqkv = sb("bqkv", [128, 48], F32)
        bo = sb("bo", [128, NFC], F32)
        sg = sb("sg", [128, NIC], F32)
        su = sb("su", [128, NIC], F32)
        ds = sb("ds", [128, NFC], F32)
        dshalf = sb("dshalf", [128, NFC], F32)

        pb = [ps(f"pb{i}", [128, 512], F32) for i in range(7)]
        pvt = ps("pvt", [128, 1024], BF16)

        semnames = ("sW", "sL", "sS", "sPE", "sA", "sV", "sP",
                    "sSL0", "sSL1", "sSL2", "sSL3", "sWd", "sOq",
                    "sKV0", "sKV1", "sKV2", "sKV3", "sKB0", "sKB1",
                    "sSt0", "sSt1", "sSt2", "sSt3",
                    "sCk0", "sCk1", "sCkV0", "sCkV1", "sCo", "sCo2", "sWo")
        sems = {n: es.enter_context(nc.semaphore(n)) for n in semnames}
        em = Em(sems)

        xw = [None] * NFC
        bankB = [Buf() for _ in range(7)]
        pvtB = [Buf(), Buf()]
        xlnB = Buf()
        hB = [Buf() for _ in range(8)]
        hKB = [Buf(), Buf()]
        wguB = [Buf() for _ in range(4)]
        wdB = [Buf() for _ in range(8)]
        tmpB = [Buf(), Buf()]
        stgB = [Buf() for _ in range(4)]
        ptB = [Buf(), Buf(), Buf(), Buf()]
        smfB = [Buf() for _ in range(3)]
        smbB = [Buf() for _ in range(5)]
        gather_done = {}

        def smf_ap(idx, rows=2):
            return smf[idx][0:rows, :]

        def smb_ap(idx, rows=2):
            return smb[idx][0:rows, :]

        SA, SB_, SC_ = 0, 1, 2
        B_RHI, B_RLO, B_MRHI, B_MRLO, B_XSUM = range(5)

        cv = None
        for dst, src in [
            (ident[:, :], ident_c.ap()), (selc2[:, :], selc2_c.ap()),
            (selc4[:, :], selc4_c.ap()), (selbc[:, :], selbc_c.ap()),
            (selr2[:, :], selr2_c.ap()), (negsel[:, :], negsel_c.ap()),
            (fc[:, :], fc_c.ap()), (masks[:, :], masks_c.ap()),
            (idxt[:, :], idxd),
            (bqkv[:, :], bqkvT), (bo[:, :], boT),
            (sg[:, :], sgT), (su[:, :], suT), (ds[:, :], dsT),
        ]:
            cv = em.dma("sync", dst, src, "sL")
        def x_load(half, first):
            src = xin if first else y
            v = None
            for c in range(NFC):
                ws = [xw[c]] if xw[c] is not None else []
                v = em.dma("sync", X[:, c * HALF:(c + 1) * HALF],
                           src[c * 128:(c + 1) * 128, half * HALF:(half + 1) * HALF],
                           "sL", waits=ws)
            for c in range(NFC):
                xw[c] = ("sL", v)

        def x_store(half):
            v = None
            for c in range(NFC):
                ws = [xw[c]] if xw[c] is not None else []
                v = em.dma("sync", y[c * 128:(c + 1) * 128, half * HALF:(half + 1) * HALF],
                           X[:, c * HALF:(c + 1) * HALF], "sS", waits=ws)
            for c in range(NFC):
                xw[c] = ("sS", v)

        def layernorm():
            stats_v = None
            for c in range(NFC):
                sl_b, sl_q = c % 2, 2 + (c % 2)
                ws = hB[sl_b].reuse_waits() + ([xw[c]] if xw[c] else [])
                vb = em.op("dve", lambda e, sl=sl_b, c=c: e.tensor_copy(
                    out=Hsb[:, sl * HALF:(sl + 1) * HALF],
                    in_=X[:, c * HALF:(c + 1) * HALF]), inc="sV", waits=ws)
                hB[sl_b].set_ready(("sV", vb))
                ws = hB[sl_q].reuse_waits() + ([xw[c]] if xw[c] else [])
                vq = em.op("act", lambda e, sl=sl_q, c=c: e.activation(
                    out=Hsb[:, sl * HALF:(sl + 1) * HALF],
                    in_=X[:, c * HALF:(c + 1) * HALF], func=AF.Square),
                    inc="sA", waits=ws)
                hB[sl_q].set_ready(("sA", vq))
                ws = [("sA", vq), ("sV", vb)]
                if c == 0:
                    ws += bankB[0].reuse_waits() + bankB[1].reuse_waits()
                for t2 in range(2):
                    em.op("pe", lambda e, t2=t2, sl=sl_b, c=c: e.matmul(
                        pb[0][0:2, :], selc2[:, 2 * t2:2 * t2 + 2],
                        Hsb[:, sl * HALF + t2 * 512: sl * HALF + (t2 + 1) * 512],
                        start=(c == 0 and t2 == 0),
                        stop=(c == NFC - 1 and t2 == 1)), waits=ws)
                    ws = []
                    last = (c == NFC - 1 and t2 == 1)
                    v = em.op("pe", lambda e, t2=t2, sl=sl_q, c=c, last=last: e.matmul(
                        pb[1][0:2, :], selc2[:, 2 * t2:2 * t2 + 2],
                        Hsb[:, sl * HALF + t2 * 512: sl * HALF + (t2 + 1) * 512],
                        start=(c == 0 and t2 == 0), stop=last),
                        inc="sPE")
                    stats_v = v
                hB[sl_b].used(("sPE", stats_v))
                hB[sl_q].used(("sPE", stats_v))
            def dve_small(fn, kind, idx, waits):
                tr = smfB if kind == "f" else smbB
                v = em.op("dve", fn, inc="sV", waits=list(waits) + tr[idx].reuse_waits())
                tr[idx].set_ready(("sV", v))
                return v

            v = dve_small(lambda e: e.tensor_copy(out=smf_ap(SA), in_=pb[0][0:2, :]),
                          "f", SA, [("sPE", stats_v)])
            bankB[0].set_ready(None); bankB[0].users = [("sV", v)]
            v = dve_small(lambda e: e.tensor_copy(out=smf_ap(SB_), in_=pb[1][0:2, :]),
                          "f", SB_, [])
            bankB[1].set_ready(None); bankB[1].users = [("sV", v)]
            dve_small(lambda e: e.tensor_scalar_mul(
                out=smf_ap(SC_), in0=smf_ap(SA),
                scalar1=fc[0:2, FC_INV2048:FC_INV2048 + 1]), "f", SC_, [])
            smfB[SA].used(("sV", em.cnt["sV"]))
            dve_small(lambda e: e.tensor_mul(
                out=smf_ap(SA), in0=smf_ap(SC_), in1=smf_ap(SC_)), "f", SA, [])
            var_v = dve_small(lambda e: e.scalar_tensor_tensor(
                out=smf_ap(SB_), in0=smf_ap(SB_),
                scalar=fc[0:2, FC_INV2048:FC_INV2048 + 1],
                in1=smf_ap(SA), op0=ALU.mult, op1=ALU.subtract), "f", SB_, [])
            smfB[SA].used(("sV", var_v))
            v = em.op("act", lambda e: e.activation(
                out=smf_ap(SA), in_=smf_ap(SB_), func=AF.Ln,
                bias=fc[0:2, FC_EPS:FC_EPS + 1]),
                inc="sA", waits=[("sV", var_v)] + smfB[SA].reuse_waits())
            r_v = em.op("act", lambda e: e.activation(
                out=smf_ap(SA), in_=smf_ap(SA), func=AF.Exp,
                scale=fc[0:2, FC_NHALF:FC_NHALF + 1]), inc="sA")
            smfB[SA].set_ready(("sA", r_v))
            smfB[SB_].used(("sA", r_v))
            dve_small(lambda e: e.tensor_mul(
                out=smf_ap(SB_), in0=smf_ap(SC_), in1=smf_ap(SA)), "f", SB_,
                [("sA", r_v)])
            smfB[SC_].used(("sV", em.cnt["sV"]))
            smfB[SA].used(("sV", em.cnt["sV"]))
            split_v = None
            for src_s, hi_b, lo_b in ((SA, B_RHI, B_RLO), (SB_, B_MRHI, B_MRLO)):
                dve_small(lambda e, src_s=src_s, hi_b=hi_b: e.tensor_copy(
                    out=smb_ap(hi_b), in_=smf_ap(src_s)), "b", hi_b, [])
                dve_small(lambda e, hi_b=hi_b: e.tensor_copy(
                    out=smf_ap(SC_), in_=smb_ap(hi_b)), "f", SC_, [])
                split_v = dve_small(lambda e, src_s=src_s, lo_b=lo_b: e.tensor_sub(
                    out=smb_ap(lo_b), in0=smf_ap(src_s), in1=smf_ap(SC_)), "b", lo_b, [])
                smfB[SC_].used(("sV", split_v))
                smfB[src_s].used(("sV", split_v))
            for t2 in range(2):
                ws = ([("sV", split_v)] + bankB[4].reuse_waits()
                      + bankB[5].reuse_waits())
                em.op("pe", lambda e, t2=t2: e.matmul(
                    pb[4][:, :], selr2[:, 128 * t2:128 * (t2 + 1)], smb_ap(B_RHI),
                    start=True, stop=False), waits=ws)
                em.op("pe", lambda e, t2=t2: e.matmul(
                    pb[4][:, :], selr2[:, 128 * t2:128 * (t2 + 1)], smb_ap(B_RLO),
                    start=False, stop=True))
                em.op("pe", lambda e, t2=t2: e.matmul(
                    pb[5][:, :], selr2[:, 128 * t2:128 * (t2 + 1)], smb_ap(B_MRHI),
                    start=True, stop=False))
                bc_v = em.op("pe", lambda e, t2=t2: e.matmul(
                    pb[5][:, :], selr2[:, 128 * t2:128 * (t2 + 1)], smb_ap(B_MRLO),
                    start=False, stop=True), inc="sPE")
                bankB[4].set_ready(("sPE", bc_v))
                bankB[5].set_ready(("sPE", bc_v))
                smbB[B_RHI].used(("sPE", bc_v))
                smbB[B_RLO].used(("sPE", bc_v))
                smbB[B_MRHI].used(("sPE", bc_v))
                smbB[B_MRLO].used(("sPE", bc_v))
                last_tt = None
                for c in range(NFC):
                    ws = [("sPE", bc_v)] + ([xw[c]] if xw[c] else [])
                    if c == 0:
                        ws += xlnB.reuse_waits() + tmpB[0].reuse_waits()
                    em.op("dve", lambda e, c=c, t2=t2: e.tensor_mul(
                        out=tmp[:, 0:512],
                        in0=X[:, c * HALF + t2 * 512: c * HALF + (t2 + 1) * 512],
                        in1=pb[4][:, :]), waits=ws)
                    last_tt = em.op("dve", lambda e, c=c, t2=t2: e.tensor_sub(
                        out=xln[:, c * HALF + t2 * 512: c * HALF + (t2 + 1) * 512],
                        in0=tmp[:, 0:512], in1=pb[5][:, :]), inc="sV")
                bankB[4].used(("sV", last_tt))
                bankB[5].used(("sV", last_tt))
                if t2 == 0:
                    xlnB.set_ready(("sV", last_tt))
                    tmpB[0].set_ready(("sV", last_tt))
                else:
                    xlnB.ready = ("sV", last_tt)
                    tmpB[0].ready = ("sV", last_tt)
            xs_v = None
            ws = bankB[2].reuse_waits() + [xlnB.ready]
            for c in range(NFC):
                for t2 in range(2):
                    last = (c == NFC - 1 and t2 == 1)
                    xs_v = em.op("pe", lambda e, c=c, t2=t2, last=last: e.matmul(
                        pb[2][0:2, :], selc2[:, 2 * t2:2 * t2 + 2],
                        xln[:, c * HALF + t2 * 512: c * HALF + (t2 + 1) * 512],
                        start=(c == 0 and t2 == 0), stop=last),
                        inc=("sPE" if last else None), waits=ws) or xs_v
                    ws = []
            v = em.op("dve", lambda e: e.tensor_copy(out=smb_ap(B_XSUM), in_=pb[2][0:2, :]),
                      inc="sV", waits=[("sPE", xs_v)] + smbB[B_XSUM].reuse_waits())
            smbB[B_XSUM].set_ready(("sV", v))
            bankB[2].set_ready(None); bankB[2].users = [("sV", v)]

        def load_gu(i):
            gslot, uslot = i % 2, 2 + (i % 2)
            for slot, src in ((gslot, wg4), (uslot, wu4)):
                ws = wguB[slot].reuse_waits()
                v = em.dma("pool", wgu[:, slot * 2048:(slot + 1) * 2048],
                           src[i].rearrange("p a b -> p (a b)"), f"sSL{slot}", waits=ws)
                wguB[slot].set_ready((f"sSL{slot}", v))

        def ffn(after_band_hook=None, early_hook=None):
            xln_r = xlnB.ready
            xsum_r = smbB[B_XSUM].ready
            load_gu(0)
            load_gu(1)
            uidx = 0
            for b in range(NB):
                dn_state = {"i": 0, "v": None}

                def emit_down(k):
                    for _ in range(k):
                        ii = dn_state["i"]
                        if ii >= BAND:
                            return
                        dn_state["i"] += 1
                        i = b * BAND + ii
                        ws = wdB[ii].reuse_waits()
                        dn_state["v"] = em.dma(
                            "pool", wd[:, ii * 2048:(ii + 1) * 2048], wd3[i],
                            "sWd", waits=ws)
                if b == 0 and early_hook is not None:
                    emit_down(BAND)
                    early_hook()
                hs_mm_v = None
                hs_pend = []

                def flush_hsum():
                    ws, ii, t2, last = hs_pend.pop(0)
                    return em.op("pe", lambda e, ii=ii, t2=t2, last=last: e.matmul(
                        pb[6][0:2, :], selc2[:, 2 * t2:2 * t2 + 2],
                        Hsb[:, ii * HALF + t2 * 512: ii * HALF + (t2 + 1) * 512],
                        start=(ii == 0 and t2 == 0), stop=last),
                        inc=("sPE" if last else None), waits=ws)

                for ii in range(BAND):
                    i = b * BAND + ii
                    gslot, uslot = i % 2, 2 + (i % 2)
                    for t2 in range(2):
                        par = uidx % 2
                        gb, ub = par, 2 + par
                        ws = [wguB[gslot].ready, xln_r, xsum_r] + bankB[gb].reuse_waits()
                        for hc in range(NFC):
                            em.op("pe", lambda e, gslot=gslot, hc=hc, t2=t2, gb=gb: e.matmul(
                                pb[gb][:, :],
                                wgu[:, gslot * 2048 + hc * 128: gslot * 2048 + (hc + 1) * 128],
                                xln[:, hc * HALF + t2 * 512: hc * HALF + (t2 + 1) * 512],
                                start=(hc == 0), stop=False), waits=ws)
                            ws = []
                        g_v = em.op("pe", lambda e, t2=t2, gb=gb: e.matmul(
                            pb[gb][:, :], negsel[:, 128 * t2:128 * (t2 + 1)],
                            smb_ap(B_XSUM), start=False, stop=True), inc="sPE")
                        bankB[gb].set_ready(("sPE", g_v))
                        wguB[gslot].used(("sPE", g_v))
                        ws = [wguB[uslot].ready] + bankB[ub].reuse_waits()
                        for hc in range(NFC):
                            em.op("pe", lambda e, uslot=uslot, hc=hc, t2=t2, ub=ub: e.matmul(
                                pb[ub][:, :],
                                wgu[:, uslot * 2048 + hc * 128: uslot * 2048 + (hc + 1) * 128],
                                xln[:, hc * HALF + t2 * 512: hc * HALF + (t2 + 1) * 512],
                                start=(hc == 0), stop=False), waits=ws)
                            ws = []
                        u_v = em.op("pe", lambda e, t2=t2, ub=ub: e.matmul(
                            pb[ub][:, :], negsel[:, 128 * t2:128 * (t2 + 1)],
                            smb_ap(B_XSUM), start=False, stop=True), inc="sPE")
                        bankB[ub].set_ready(("sPE", u_v))
                        wguB[uslot].used(("sPE", u_v))
                        tslot = par
                        ws = [("sPE", g_v)] + tmpB[tslot].reuse_waits()
                        silu_v = em.op("act", lambda e, gb=gb, i=i, tslot=tslot: e.activation(
                            out=tmp[:, tslot * 512:(tslot + 1) * 512], in_=pb[gb][:, :],
                            func=AF.Silu, scale=sg[:, i:i + 1]), inc="sA", waits=ws)
                        tmpB[tslot].set_ready(("sA", silu_v))
                        bankB[gb].used(("sA", silu_v))
                        ws = [("sA", silu_v), ("sPE", u_v)] + hB[ii].reuse_waits()
                        h_v = em.op("dve", lambda e, ub=ub, i=i, ii=ii, t2=t2, tslot=tslot:
                                    e.scalar_tensor_tensor(
                                        out=Hsb[:, ii * HALF + t2 * 512: ii * HALF + (t2 + 1) * 512],
                                        in0=pb[ub][:, :], scalar=su[:, i:i + 1],
                                        in1=tmp[:, tslot * 512:(tslot + 1) * 512],
                                        op0=ALU.mult, op1=ALU.mult), inc="sV", waits=ws)
                        if t2 == 0:
                            hB[ii].set_ready(("sV", h_v))
                        else:
                            hB[ii].ready = ("sV", h_v)
                        bankB[ub].used(("sV", h_v))
                        tmpB[tslot].used(("sV", h_v))
                        ws = [("sV", h_v)]
                        if ii == 0 and t2 == 0:
                            ws += bankB[6].reuse_waits()
                        last = (ii == BAND - 1 and t2 == 1)
                        hs_pend.append((ws, ii, t2, last))
                        if len(hs_pend) > 2:
                            hs_mm_v = flush_hsum() or hs_mm_v
                        uidx += 1
                        if t2 == 1 and i + 2 < NIC and (i + 2) // BAND == b:
                            load_gu(i + 2)
                    emit_down(2)
                    if after_band_hook is not None:
                        after_band_hook(b)
                    if ii == BAND - 1 and b + 1 < NB:
                        load_gu(b * BAND + BAND)
                        load_gu(b * BAND + BAND + 1)
                while hs_pend:
                    hs_mm_v = flush_hsum() or hs_mm_v
                emit_down(BAND)
                bankB[6].set_ready(("sPE", hs_mm_v))
                v = em.op("dve", lambda e: e.tensor_copy(out=smf_ap(SA), in_=pb[6][0:2, :]),
                          inc="sV", waits=[("sPE", hs_mm_v)] + smfB[SA].reuse_waits())
                smfB[SA].set_ready(("sV", v))
                bankB[6].used(("sV", v))
                v = em.op("dve", lambda e: e.tensor_copy(out=smb_ap(B_RHI), in_=smf_ap(SA)),
                          inc="sV", waits=smbB[B_RHI].reuse_waits())
                smbB[B_RHI].set_ready(("sV", v))
                v = em.op("dve", lambda e: e.tensor_copy(out=smf_ap(SB_), in_=smb_ap(B_RHI)),
                          inc="sV", waits=smfB[SB_].reuse_waits())
                smfB[SB_].set_ready(("sV", v))
                hs_v = em.op("dve", lambda e: e.tensor_sub(
                    out=smb_ap(B_RLO), in0=smf_ap(SA), in1=smf_ap(SB_)),
                    inc="sV", waits=smbB[B_RLO].reuse_waits())
                smbB[B_RLO].set_ready(("sV", hs_v))
                smfB[SA].used(("sV", hs_v))
                smfB[SB_].used(("sV", hs_v))
                for ii in range(BAND):
                    wdB[ii].set_ready(("sWd", dn_state["v"]))
                vidx = 0
                for hf in range(NFC):
                    for t2 in range(2):
                        yb = 4 + (vidx % 2)
                        ws = bankB[yb].reuse_waits()
                        for ii in range(BAND):
                            ws2 = ws + [wdB[ii].ready, hB[ii].ready]
                            em.op("pe", lambda e, ii=ii, hf=hf, t2=t2, yb=yb: e.matmul(
                                pb[yb][:, :],
                                wd[:, ii * 2048 + hf * 128: ii * 2048 + (hf + 1) * 128],
                                Hsb[:, ii * HALF + t2 * 512: ii * HALF + (t2 + 1) * 512],
                                start=(ii == 0), stop=False), waits=ws2)
                            ws = []
                        em.op("pe", lambda e, t2=t2, yb=yb: e.matmul(
                            pb[yb][:, :], negsel[:, 128 * t2:128 * (t2 + 1)],
                            smb_ap(B_RHI), start=False, stop=False),
                            waits=[("sV", hs_v)])
                        y_v = em.op("pe", lambda e, t2=t2, yb=yb: e.matmul(
                            pb[yb][:, :], negsel[:, 128 * t2:128 * (t2 + 1)],
                            smb_ap(B_RLO), start=False, stop=True), inc="sPE")
                        bankB[yb].set_ready(("sPE", y_v))
                        for ii in range(BAND):
                            wdB[ii].used(("sPE", y_v))
                            hB[ii].used(("sPE", y_v))
                        smbB[B_RHI].used(("sPE", y_v))
                        smbB[B_RLO].used(("sPE", y_v))
                        ws = [("sPE", y_v)] + ([xw[hf]] if xw[hf] else [])
                        xv = em.op("dve", lambda e, hf=hf, t2=t2, yb=yb: e.scalar_tensor_tensor(
                            out=X[:, hf * HALF + t2 * 512: hf * HALF + (t2 + 1) * 512],
                            in0=pb[yb][:, :], scalar=dshalf[:, hf:hf + 1],
                            in1=X[:, hf * HALF + t2 * 512: hf * HALF + (t2 + 1) * 512],
                            op0=ALU.mult, op1=ALU.add), inc="sV", waits=ws)
                        xw[hf] = ("sV", xv)
                        bankB[yb].used(("sV", xv))
                        vidx += 1
            blanket = ("sPE", em.cnt["sPE"])
            xlnB.used(blanket)
            smbB[B_XSUM].used(blanket)

        def qkv(g, defer_gather=False):
            xln_r = xlnB.ready
            order = list(range(16, 32)) + list(range(0, 16)) + list(range(32, 48))
            uidx = 0
            vt_pend = []

            def flush_vt():
                vu, ev_v, st_slot, head, t2 = vt_pend.pop(0)
                ph = vu % 2
                ws = [("sA", ev_v)] + pvtB[ph].reuse_waits()
                tp_v = None
                for j in range(4):
                    tp_v = em.op("pe", lambda e, st_slot=st_slot, j=j, ph=ph: e.transpose(
                        pvt[:, ph * 512 + j * 128: ph * 512 + (j + 1) * 128],
                        stg[:, st_slot * 512 + j * 128: st_slot * 512 + (j + 1) * 128],
                        ident[:, :]),
                        inc=("sPE" if j == 3 else None), waits=ws) or tp_v
                    ws = []
                pvtB[ph].set_ready(("sPE", tp_v))
                stgB[st_slot].used(("sPE", tp_v))
                st2 = (vu + 2) % 4
                ws = [("sPE", tp_v)] + stgB[st2].reuse_waits()
                e2_v = em.op("dve", lambda e, ph=ph, st2=st2: e.tensor_copy(
                    out=stg[:, st2 * 512:(st2 + 1) * 512],
                    in_=pvt[:, ph * 512:(ph + 1) * 512]), inc="sV", waits=ws)
                stgB[st2].set_ready(("sV", e2_v))
                pvtB[ph].used(("sV", e2_v))
                fshard, f0 = head // 8, (head % 8) * 128
                dst = (kvqv_own[g].ap()
                       [fshard * HALF + t2 * 512:
                        fshard * HALF + (t2 + 1) * 512, f0:f0 + 128]
                       .rearrange("(j p) d -> p j d", p=128))
                dv = em.dma("sync", dst, stg[:, st2 * 512:(st2 + 1) * 512],
                            f"sSt{st2}", waits=[("sV", e2_v)])
                stgB[st2].used((f"sSt{st2}", dv))

            for n, oc in enumerate(order):
                slot = n % 4
                ws = wguB[slot].reuse_waits()
                v = em.dma("pool", wgu[:, slot * 2048:(slot + 1) * 2048],
                           wqkv4[oc].rearrange("p a b -> p (a b)"), f"sSL{slot}", waits=ws)
                wguB[slot].set_ready((f"sSL{slot}", v))
                for t2 in range(2):
                    par = uidx % 2
                    ws = [wguB[slot].ready, xln_r] + bankB[par].reuse_waits()
                    mm_v = None
                    for hc in range(NFC):
                        mm_v = em.op("pe", lambda e, slot=slot, hc=hc, t2=t2, par=par: e.matmul(
                            pb[par][:, :],
                            wgu[:, slot * 2048 + hc * 128: slot * 2048 + (hc + 1) * 128],
                            xln[:, hc * HALF + t2 * 512: hc * HALF + (t2 + 1) * 512],
                            start=(hc == 0), stop=(hc == NFC - 1)),
                            inc=("sPE" if hc == NFC - 1 else None), waits=ws) or mm_v
                        ws = []
                    bankB[par].set_ready(("sPE", mm_v))
                    wguB[slot].used(("sPE", mm_v))
                    st_slot = uidx % 4
                    ws = [("sPE", mm_v)] + stgB[st_slot].reuse_waits()
                    ev_v = em.op("act", lambda e, par=par, oc=oc, st_slot=st_slot: e.activation(
                        out=stg[:, st_slot * 512:(st_slot + 1) * 512], in_=pb[par][:, :],
                        func=AF.Identity, bias=bqkv[:, oc:oc + 1]), inc="sA", waits=ws)
                    stgB[st_slot].set_ready(("sA", ev_v))
                    bankB[par].used(("sA", ev_v))
                    which, head = divmod(oc, 16)
                    if which in (0, 1):
                        base = (0 if which == 1 else 2048) + head * 128
                        dv = em.dma("sync",
                                    kvq_own[g].ap()[base:base + 128,
                                                    t2 * 512:(t2 + 1) * 512],
                                    stg[:, st_slot * 512:(st_slot + 1) * 512],
                                    f"sSt{st_slot}", waits=[("sA", ev_v)])
                        stgB[st_slot].used((f"sSt{st_slot}", dv))
                    else:
                        vt_pend.append((uidx, ev_v, st_slot, head, t2))
                        if len(vt_pend) > 1:
                            flush_vt()
                    uidx += 1
                if n == 31:
                    stw_kq = [(f"sSt{j}", em.cnt[f"sSt{j}"]) for j in range(4)]
                if n == 47:
                    while vt_pend:
                        flush_vt()
            stw = [(f"sSt{j}", em.cnt[f"sSt{j}"]) for j in range(4)]

            def emit_gather(queue="pool"):
                cv = em.op(queue, lambda e, g=g: e.collective_compute(
                    "AllGather", ALU.bypass, replica_groups=RG,
                    ins=[kvq_own[g].ap()], outs=[kvq_g[g].ap()]),
                    inc=f"sCk{g}", waits=stw_kq)
                gather_done[("kvq", g)] = (f"sCk{g}", cv)
                cv = em.op(queue, lambda e, g=g: e.collective_compute(
                    "AllGather", ALU.bypass, replica_groups=RG,
                    ins=[kvqv_own[g].ap()], outs=[kvqv_g[g].ap()]),
                    inc=f"sCkV{g}", waits=stw)
                gather_done[("kvqV", g)] = (f"sCkV{g}", cv)
            if not defer_gather:
                emit_gather()
            xlnB.used(("sPE", em.cnt["sPE"]))
            return emit_gather

        def kc_map(kc):
            return ((kc % 4) // 2, kc // 4, (kc % 2) * 512)

        def attention(pre_hook=None, mid_hook=None):
            pvtf = pvt.ap().bitcast(F32)
            lB = pvtB[0]
            pair_order = [(0, 0), (0, 1), (1, 0), (1, 1)]
            pair_order += [(0, qs) for qs in range(2, 8)]
            pair_order += [(1, qs) for qs in range(2, 8)]

            def igather(queue, out_ap, g, col, eoff, sem, waits, vten=False):
                em.waits(queue, waits)
                em.cnt[sem] += 16
                h = em.sems[sem]
                flat = (kvqv_g if vten else kvq_g)[g].ap()
                em.ops[queue].append(
                    lambda e, out_ap=out_ap, flat=flat, col=col, eoff=eoff, h=h:
                    e.indirect_dma_start(
                        out=out_ap, out_offset=None, in_=flat,
                        in_offset=IndirectOffsetOnAxis(ap=idxt[:, col:col + 1], axis=0),
                        element_offset=eoff).then_inc(h, 16))
                return em.cnt[sem]

            def emit_q(gi):
                hg, qs = pair_order[2 * gi]
                gq, jq, _ = kc_map(qs)
                P = gi % 2
                ws = (wguB[2 * P].reuse_waits() + wguB[2 * P + 1].reuse_waits()
                      + [gather_done[("kvq", gq)]])
                qv = None
                for s in range(4):
                    qv = igather("pool",
                                 wgu[:, P * 4096 + s * 1024: P * 4096 + (s + 1) * 1024],
                                 gq, 16 + jq * 8 + hg * 4 + s, 0, f"sSL{2 * P}", ws)
                    ws = []
                wguB[2 * P].set_ready((f"sSL{2 * P}", qv))
                wguB[2 * P + 1].set_ready((f"sSL{2 * P}", qv))

            q_emitted = set()

            def maybe_emit_q(pi):
                gi = pi // 2
                if 2 * gi < len(pair_order) and gi not in q_emitted:
                    g_need = kc_map(pair_order[2 * gi][1])[0]
                    if ("kvq", g_need) in gather_done:
                        q_emitted.add(gi)
                        emit_q(gi)

            for blk in range(2):
                us = []
                for sl in range(4 * blk, 4 * blk + 4):
                    us += hB[sl].reuse_waits()
                hKB[blk].users = us
                hKB[blk].ready = None

            fin_pend = []

            def flush_fin():
                while fin_pend:
                    fin_pend.pop(0)()

            def make_finalize(av_v, hg, qs, pi):
                def fin():
                    ws = [("sPE", av_v)] + smfB[SA].reuse_waits()
                    v = em.op("dve", lambda e: e.tensor_copy(
                        out=smf_ap(SA, rows=4), in_=pvtf[0:4, 0:512]),
                        inc="sV", waits=ws)
                    smfB[SA].set_ready(("sV", v))
                    lB.used(("sV", v))
                    v = em.op("dve", lambda e: e.reciprocal(
                        out=smf_ap(SB_, rows=4), in_=smf_ap(SA, rows=4)),
                        inc="sV", waits=smfB[SB_].reuse_waits())
                    smfB[SB_].set_ready(("sV", v))
                    smfB[SA].used(("sV", v))
                    lb = B_RHI
                    lb_v = em.op("dve", lambda e, lb=lb: e.tensor_copy(
                        out=smb_ap(lb, rows=4), in_=smf_ap(SB_, rows=4)),
                        inc="sV", waits=smbB[lb].reuse_waits())
                    smbB[lb].set_ready(("sV", lb_v))
                    smfB[SB_].used(("sV", lb_v))
                    for s in range(4):
                        bcb = s % 2
                        ws = [("sV", lb_v)] + bankB[bcb].reuse_waits()
                        bc_v = em.op("pe", lambda e, s=s, bcb=bcb, lb=lb: e.matmul(
                            pb[bcb][:, :], selbc[:, 128 * s:128 * (s + 1)],
                            smb_ap(lb, rows=4), start=True, stop=True),
                            inc="sPE", waits=ws)
                        bankB[bcb].set_ready(("sPE", bc_v))
                        smbB[lb].used(("sPE", bc_v))
                        tslot = s % 2
                        ws = [("sPE", bc_v)] + tmpB[tslot].reuse_waits()
                        cp_v = em.op("dve", lambda e, bcb=bcb, tslot=tslot: e.tensor_copy(
                            out=tmp[:, tslot * 512:(tslot + 1) * 512], in_=pb[bcb][:, :]),
                            inc="sV", waits=ws)
                        tmpB[tslot].set_ready(("sV", cp_v))
                        bankB[bcb].used(("sV", cp_v))
                        st_slot = s
                        ob = 3 + s
                        ws = stgB[st_slot].reuse_waits()
                        nm_v = em.op("dve", lambda e, ob=ob, tslot=tslot, st_slot=st_slot: e.tensor_mul(
                            out=stg[:, st_slot * 512:(st_slot + 1) * 512],
                            in0=pb[ob][:, :], in1=tmp[:, tslot * 512:(tslot + 1) * 512]),
                            inc="sV", waits=ws)
                        stgB[st_slot].set_ready(("sV", nm_v))
                        tmpB[tslot].used(("sV", nm_v))
                        bankB[ob].used(("sV", nm_v))
                        oo = o_own_a if hg == 0 else o_own_b
                        dv = em.dma("sync",
                                    oo.ap()[qs // 4, s][:, (qs % 4) * 512:(qs % 4) * 512 + 512],
                                    stg[:, st_slot * 512:(st_slot + 1) * 512],
                                    f"sSt{st_slot}", waits=[("sV", nm_v)])
                        stgB[st_slot].used((f"sSt{st_slot}", dv))
                    if pi == 9:
                        stw = [(f"sSt{j}", em.cnt[f"sSt{j}"]) for j in range(4)]
                        cv = em.op("pool", lambda e: e.collective_compute(
                            "AllGather", ALU.bypass, replica_groups=RG,
                            ins=[o_own_a.ap()], outs=[o_ga.ap()]),
                            inc="sCo", waits=stw)
                        gather_done[("o", "a")] = ("sCo", cv)
                return fin

            cur_kb = {}
            cur_v = {}

            def load_kblock(kb, hgl, gk, jk):
                if cur_kb.get(kb) == (hgl, gk, jk):
                    return
                ws = hKB[kb].reuse_waits() + [gather_done[("kvq", gk)]]
                kkv = None
                for s in range(4):
                    kkv = igather(
                        "pool",
                        Hsb[:, kb * 4096 + s * 1024: kb * 4096 + (s + 1) * 1024],
                        gk, jk * 8 + hgl * 4 + s, 0, f"sKB{kb}", ws)
                    ws = []
                hKB[kb].set_ready((f"sKB{kb}", kkv))
                cur_kb[kb] = (hgl, gk, jk)

            def load_v(r, hgl, gk, jk, ckh):
                if cur_v.get(r) == (hgl, gk, jk, ckh):
                    return
                vsl = (2 * r, 2 * r + 1)
                ws = (wdB[vsl[0]].reuse_waits() + wdB[vsl[1]].reuse_waits()
                      + [gather_done[("kvqV", gk)]])
                kv = None
                for ks in range(4):
                    kv = igather("pool",
                                 wd[:, r * 2048 + ks * 512: r * 2048 + (ks + 1) * 512],
                                 gk, 32 + jk * 8 + ckh * 4 + ks,
                                 hgl * 512, f"sKV{r}", ws, vten=True)
                    ws = []
                for sl in vsl:
                    wdB[sl].set_ready((f"sKV{r}", kv))
                cur_v[r] = (hgl, gk, jk, ckh)

            for hgE in (0, 1):
                load_kblock(hgE, hgE, 0, 0)
                maybe_emit_q(2 * hgE)
                for ckh in (0, 1):
                    load_v(2 * hgE + ckh, hgE, 0, 0, ckh)
            if pre_hook is not None:
                pre_hook()
            for pi, (hg, qs) in enumerate(pair_order):
                if mid_hook is not None and pi in mid_hook:
                    mid_hook[pi]()
                maybe_emit_q(pi)
                P2 = (pi // 2) % 2
                cq2 = (qs % 2) * 512
                tile = 0
                nkc = qs + 1
                avl_pend = []
                lv_last = None

                def flush_avl():
                    nonlocal lv_last
                    (ex_v, r, ks, s, ptp, first, last) = avl_pend.pop(0)
                    ob = 3 + s
                    ws = [("sA", ex_v)]
                    if first:
                        ws = ws + bankB[ob].reuse_waits()
                        if s == 0:
                            ws = ws + lB.reuse_waits()
                    em.op("pe", lambda e, r=r, ks=ks, s=s, ptp=ptp, first=first, last=last: e.matmul(
                        pb[3 + s][:, :],
                        wd[:, r * 2048 + ks * 512 + s * 128: r * 2048 + ks * 512 + (s + 1) * 128],
                        ptb[:, ptp * 512:(ptp + 1) * 512],
                        start=first, stop=last), waits=ws)
                    l_v = em.op("pe", lambda e, s=s, ptp=ptp, first=first, last=last: e.matmul(
                        pvtf[0:4, 0:512], selc4[:, 4 * s:4 * s + 4],
                        ptb[:, ptp * 512:(ptp + 1) * 512],
                        start=(first and s == 0), stop=(last and s == 3)),
                        inc="sPE")
                    ptB[ptp].used(("sPE", l_v))
                    wdB[2 * r].used(("sPE", l_v))
                    wdB[2 * r + 1].used(("sPE", l_v))
                    if first:
                        bankB[ob].set_ready(("sPE", l_v))
                        if s == 0:
                            lB.set_ready(("sPE", l_v))
                    lv_last = l_v

                for kc in range(nkc):
                    gk, jk, ck = kc_map(kc)
                    if pi < 4:
                        kb = hg
                        r = 2 * hg + (kc % 2)
                    else:
                        kb = (kc // 2) % 2
                        r = kc % 4
                        load_kblock(kb, hg, gk, jk)
                        load_v(r, hg, gk, jk, ck // 512)
                    vslots = (2 * r, 2 * r + 1)
                    if kc == min(1, nkc - 1):
                        maybe_emit_q(pi + 1)
                    diag = (kc == qs)
                    for ks in range(4):
                        for s in range(4):
                            if tile == 3:
                                flush_fin()
                            par = tile % 3
                            ws = ([hKB[kb].ready, wguB[2 * P2].ready, wdB[vslots[0]].ready]
                                  + bankB[par].reuse_waits())
                            smm_v = em.op("pe", lambda e, kb=kb, ck=ck, s=s, ks=ks,
                                          P2=P2, cq2=cq2, par=par: e.matmul(
                                pb[par][:, :],
                                Hsb[:, kb * 4096 + s * 1024 + ck + ks * 128:
                                    kb * 4096 + s * 1024 + ck + (ks + 1) * 128],
                                wgu[:, P2 * 4096 + s * 1024 + cq2:
                                    P2 * 4096 + s * 1024 + cq2 + 512],
                                start=True, stop=True), inc="sPE", waits=ws)
                            bankB[par].set_ready(("sPE", smm_v))
                            exp_w = [("sPE", smm_v)]
                            if diag:
                                mk_v = em.op("dve", lambda e, par=par, ks=ks: e.tensor_add(
                                    out=pb[par][:, :], in0=pb[par][:, :],
                                    in1=masks[:, ks * 512:(ks + 1) * 512]),
                                    inc="sV", waits=[("sPE", smm_v)])
                                exp_w = [("sV", mk_v)]
                            ptp = tile % 4
                            ws = exp_w + ptB[ptp].reuse_waits()
                            ex_v = em.op("act", lambda e, par=par, ptp=ptp: e.activation(
                                out=ptb[:, ptp * 512:(ptp + 1) * 512], in_=pb[par][:, :],
                                func=AF.Exp, scale=fc[:, FC_SSCALE:FC_SSCALE + 1],
                                bias=fc[:, FC_CSHIFT:FC_CSHIFT + 1]),
                                inc="sA", waits=ws)
                            ptB[ptp].set_ready(("sA", ex_v))
                            bankB[par].used(("sA", ex_v))
                            first = (kc == 0 and ks == 0)
                            last = (kc == nkc - 1 and ks == 3)
                            avl_pend.append((ex_v, r, ks, s, ptp, first, last))
                            if len(avl_pend) > 3:
                                flush_avl()
                            tile += 1
                    hold = ("sPE", em.cnt["sPE"])
                    hKB[kb].used(hold)
                    for sl in vslots:
                        wdB[sl].used(hold)
                while avl_pend:
                    flush_avl()
                hold = ("sPE", em.cnt["sPE"])
                wguB[2 * P2].used(hold)
                wguB[2 * P2 + 1].used(hold)
                fin_pend.append(make_finalize(lv_last, hg, qs, pi))
            flush_fin()
            blanket = ("sPE", em.cnt["sPE"])
            for sl in range(8):
                hB[sl].used(blanket)
                wdB[sl].used(blanket)
            wo_oq_loads(0, True)
            stw = [(f"sSt{j}", em.cnt[f"sSt{j}"]) for j in range(4)]
            cv = em.op("pool", lambda e: e.collective_compute(
                "AllGather", ALU.bypass, replica_groups=RG,
                ins=[o_own_b.ap()], outs=[o_gb.ap()]),
                inc="sCo2", waits=stw)
            gather_done[("o", "b")] = ("sCo2", cv)

        oq_B = [Buf() for _ in range(32)]

        oq_loaded = {}
        wosB = [Buf(), Buf()]

        def wo_oq_loads(half, part_a):
            if (half, part_a) in oq_loaded:
                return
            oq_loaded[(half, part_a)] = True
            if half not in oq_loaded:
                oq_loaded[half] = True
                for slot in range(32):
                    b2 = Buf()
                    b2.users = wdB[slot // 4].reuse_waits() + oq_B[slot].reuse_waits()
                    oq_B[slot] = b2
            v = None
            for qli in range(2):
                ql = half * 2 + qli
                for fcx in range(NFC):
                    if ((fcx % 8) < 4) != part_a:
                        continue
                    slot = qli * 16 + fcx
                    og = o_ga if part_a else o_gb
                    gw = gather_done[("o", "a" if part_a else "b")]
                    ws = oq_B[slot].reuse_waits() + [gw]
                    em.waits("pool", ws)
                    em.cnt["sOq"] += 16
                    h = em.sems["sOq"]
                    flat = og.ap().rearrange("a b p t -> (a b p) t")
                    em.ops["pool"].append(
                        lambda e, slot=slot, flat=flat, fcx=fcx, ql=ql, h=h:
                        e.indirect_dma_start(
                            out=wd[:, slot * 512:(slot + 1) * 512], out_offset=None,
                            in_=flat,
                            in_offset=IndirectOffsetOnAxis(ap=idxt[:, 48 + fcx:49 + fcx], axis=0),
                            element_offset=ql * 512).then_inc(h, 16))
                    v = em.cnt["sOq"]
            for qli in range(2):
                for fcx in range(NFC):
                    if ((fcx % 8) < 4) == part_a:
                        oq_B[qli * 16 + fcx].set_ready(("sOq", v))

        def wo_proj(half):
            wpre = {}
            if half == 1:
                for ofc in range(2):
                    wslot = 2 + ofc
                    ws = wguB[wslot].reuse_waits()
                    v = em.dma("pool", wgu[:, wslot * 2048:(wslot + 1) * 2048],
                               wo4[ofc].rearrange("p a b -> p (a b)"), f"sSL{wslot}", waits=ws)
                    wguB[wslot].set_ready((f"sSL{wslot}", v))
                    wpre[ofc] = True
            wo_oq_loads(half, True)
            wo_oq_loads(half, False)
            passes = ([[f for f in range(NFC) if (f % 8) < 4],
                       [f for f in range(NFC) if (f % 8) >= 4]]
                      if half == 0 else [list(range(NFC))])
            for pn, fcx_list in enumerate(passes):
                stage1 = (half == 0 and pn == 0)
                for ofc in range(NFC):
                    wslot = 2 + (ofc % 2)
                    if ofc in wpre:
                        pass
                    elif stage1:
                        st = ofc % 2
                        ws = wosB[st].reuse_waits()
                        v = None
                        for k, f0 in enumerate((0, 8)):
                            v = em.dma("sync", wos[:, st * 1024 + k * 512: st * 1024 + (k + 1) * 512],
                                       wo4[ofc][:, f0:f0 + 4, :].rearrange("p a b -> p (a b)"),
                                       "sWo", waits=ws)
                            ws = []
                        wosB[st].set_ready(("sWo", v))
                        ws = [wosB[st].ready] + wguB[wslot].reuse_waits()
                        cv = em.op("dve", lambda e, st=st, wslot=wslot: e.tensor_copy(
                            out=wgu[:, wslot * 2048: wslot * 2048 + 1024],
                            in_=wos[:, st * 1024:(st + 1) * 1024]),
                            inc="sV", waits=ws)
                        wguB[wslot].set_ready(("sV", cv))
                        wosB[st].used(("sV", cv))
                    else:
                        ws = wguB[wslot].reuse_waits()
                        v = em.dma("pool", wgu[:, wslot * 2048:(wslot + 1) * 2048],
                                   wo4[ofc].rearrange("p a b -> p (a b)"), f"sSL{wslot}", waits=ws)
                        wguB[wslot].set_ready((f"sSL{wslot}", v))
                    for qli in range(2):
                        t2 = qli
                        par = (ofc * 2 + qli) % 2
                        ws = [wguB[wslot].ready] + bankB[par].reuse_waits()
                        mm_v = None
                        nlast = len(fcx_list) - 1
                        for nf, fcx in enumerate(fcx_list):
                            slot = qli * 16 + fcx
                            wcol = (wslot * 2048 + nf * 128 if stage1
                                    else wslot * 2048 + fcx * 128)
                            ws2 = ws + [oq_B[slot].ready]
                            mm_v = em.op("pe", lambda e, wcol=wcol, slot=slot, par=par,
                                         nf=nf, nlast=nlast: e.matmul(
                                pb[par][:, :],
                                wgu[:, wcol: wcol + 128],
                                wd[:, slot * 512:(slot + 1) * 512],
                                start=(nf == 0), stop=(nf == nlast)),
                                inc=("sPE" if nf == nlast else None), waits=ws2) or mm_v
                            ws = []
                        bankB[par].set_ready(("sPE", mm_v))
                        wguB[wslot].used(("sPE", mm_v))
                        for fcx in fcx_list:
                            oq_B[qli * 16 + fcx].used(("sPE", mm_v))
                        ws = [("sPE", mm_v)] + ([xw[ofc]] if xw[ofc] else [])
                        if pn == 0:
                            xv = em.op("dve", lambda e, ofc=ofc, t2=t2, par=par: e.scalar_tensor_tensor(
                                out=X[:, ofc * HALF + t2 * 512: ofc * HALF + (t2 + 1) * 512],
                                in0=pb[par][:, :], scalar=bo[:, ofc:ofc + 1],
                                in1=X[:, ofc * HALF + t2 * 512: ofc * HALF + (t2 + 1) * 512],
                                op0=ALU.add, op1=ALU.add), inc="sV", waits=ws)
                        else:
                            xv = em.op("dve", lambda e, ofc=ofc, t2=t2, par=par: e.tensor_add(
                                out=X[:, ofc * HALF + t2 * 512: ofc * HALF + (t2 + 1) * 512],
                                in0=pb[par][:, :],
                                in1=X[:, ofc * HALF + t2 * 512: ofc * HALF + (t2 + 1) * 512]),
                                inc="sV", waits=ws)
                        xw[ofc] = ("sV", xv)
                        bankB[par].used(("sV", xv))
            blanket = ("sPE", em.cnt["sPE"])
            for ii in range(8):
                wdB[ii].used(blanket)

        do_ffn1 = "ffn1" in stages
        do_attn = "attn" in stages
        do_ffn2 = "ffn2" in stages

        x_load(0, first=True)
        em.op("dve", lambda e: e.tensor_scalar_mul(
            out=dshalf[:, :], in0=ds[:, :], scalar1=fc[:, FC_HALF:FC_HALF + 1]),
            inc="sV", waits=[("sL", em.cnt["sL"])])
        if do_ffn1:
            layernorm()
            ffn()
        if do_attn:
            layernorm()
            x_store(0)
            x_load(1, first=True)
            eg0 = qkv(0, defer_gather=True)
        else:
            x_store(0)
            x_load(1, first=True)
        if do_ffn1:
            layernorm()
            ffn(early_hook=(eg0 if do_attn else None))
        elif do_attn:
            eg0()
        if do_attn:
            layernorm()
            eg1 = qkv(1, defer_gather=True)
            attention(pre_hook=eg1,
                      mid_hook={4: lambda: (x_store(1),
                                            x_load(0, first=False))})
            wo_proj(0)
        else:
            x_store(1)
            x_load(0, first=False)
        if do_ffn2:
            layernorm()
            ffn()
        x_store(0)
        x_load(1, first=False)
        if do_attn:
            wo_proj(1)
        if do_ffn2:
            layernorm()
            ffn()
        x_store(1)
        for sname in semnames:
            em.wait("sync", (sname, em.cnt[sname]))

        n_ops = sum(len(v) for v in em.ops.values())

        block = es.enter_context(nc.Block())
        block.tensor(lambda e: [fn(e) for fn in em.ops["pe"]])
        block.scalar(lambda e: [fn(e) for fn in em.ops["act"]])
        block.vector(lambda e: [fn(e) for fn in em.ops["dve"]])
        block.gpsimd(lambda e: [fn(e) for fn in em.ops["pool"]])
        block.sync(lambda e: [fn(e) for fn in em.ops["sync"]])

    return nc, n_ops


def _prep_maps(inputs):
    gate_w = np.ascontiguousarray(np.asarray(inputs["L0_ffn_gate_w"], np.int32).T)
    up_w = np.ascontiguousarray(np.asarray(inputs["L0_ffn_up_w"], np.int32).T)
    down_w = np.ascontiguousarray(np.asarray(inputs["L0_ffn_down_w"], np.int32).T)
    shared = {
        "wg4": np.ascontiguousarray(gate_w.reshape(16, 128, 64, 128).transpose(2, 1, 0, 3)),
        "wu4": np.ascontiguousarray(up_w.reshape(16, 128, 64, 128).transpose(2, 1, 0, 3)),
        "wd3": np.ascontiguousarray(down_w.reshape(64, 128, 2048)),
        "wqkv4": np.ascontiguousarray(
            np.asarray(inputs["L0_attn_Wqkv"], np.float32).reshape(16, 128, 48, 128).transpose(2, 1, 0, 3)),
        "wo4": np.ascontiguousarray(
            np.asarray(inputs["L0_attn_Wo"], np.float32).reshape(16, 128, 16, 128).transpose(2, 1, 0, 3)),
        "bqkvT": np.ascontiguousarray(np.asarray(inputs["L0_attn_bqkv"], np.float32).reshape(48, 128).T),
        "boT": np.ascontiguousarray(np.asarray(inputs["L0_attn_bo"], np.float32).reshape(16, 128).T),
        "sgT": np.ascontiguousarray(np.asarray(inputs["L0_ffn_gate_scale"], np.float32).reshape(64, 128).T),
        "suT": np.ascontiguousarray(np.asarray(inputs["L0_ffn_up_scale"], np.float32).reshape(64, 128).T),
        "dsT": np.ascontiguousarray(np.asarray(inputs["L0_ffn_down_scale"], np.float32).reshape(16, 128).T),
    }
    x = np.asarray(inputs["L0_attn_x"], np.float32)
    p = np.arange(128, dtype=np.int32)
    in_maps = []
    for c in range(8):
        b, me = c // 2, c % 2
        idx = np.zeros((128, 64), dtype=np.int32)
        for j in range(2):
            for sm in range(8):
                idx[:, j * 8 + sm] = j * 4096 + (8 * me + sm) * 128 + p
                idx[:, 16 + j * 8 + sm] = j * 4096 + 2048 + (8 * me + sm) * 128 + p
            for ckh in range(2):
                for ks in range(4):
                    idx[:, 32 + j * 8 + ckh * 4 + ks] = (
                        j * 2048 + me * 1024 + ckh * 512 + ks * 128 + p)
        for fc in range(16):
            idx[:, 48 + fc] = ((fc // 8) * 2 + me) * 512 + (fc % 8 % 4) * 128 + p
        m = dict(shared)
        m["xin"] = np.ascontiguousarray(x[b].T[:, me * T:(me + 1) * T])
        m["idxd"] = idx
        in_maps.append(m)
    return in_maps


_CACHED = {}


def run_device(inputs, stages=STAGES_FULL):
    key = tuple(stages)
    if key not in _CACHED:
        _CACHED[key] = build_program(stages)
    nc, _ = _CACHED[key]
    in_maps = _prep_maps(inputs)
    res = run_bass_kernel_spmd(nc, in_maps, core_ids=list(range(8)))
    out = np.empty((4, 4096, 2048), dtype=np.float32)
    for c in range(8):
        b, half = c // 2, c % 2
        out[b, half * T:(half + 1) * T, :] = np.asarray(res.results[c]["y"]).T
    return out


def kernel(**inputs):
    return run_device(inputs, STAGES_FULL)
